# version 04
# baseline (speedup 1.0000x reference)
import bisect
import numpy as np
import concourse.bass as bass
import concourse.mybir as mybir
from concourse.bass_utils import run_bass_kernel_spmd

F32 = mybir.dt.float32
BF16 = mybir.dt.bfloat16
AL = mybir.AluOpType
AF = mybir.ActivationFunctionType
AX = mybir.AxisListType.X

D = 2048
L = 4096
NCH = 16
LS = 512
NSEG = L // LS
LSH = LS + 4
IG = 1408
IGC = 11
NGRP = 12
EPS = 1e-5

QSLOTS = {"SP": 12, "ACTQ": 12, "ROWQ": 4}
QENG = {"SP": "SP", "ACTQ": "ACT", "ROWQ": "ACT"}
DOMS = ["PE", "ACT", "DVE", "PL"] + [f"{q}{i}" for q, n in QSLOTS.items()
                                     for i in range(n)]
DOM2ENG = {"PE": "PE", "ACT": "ACT", "DVE": "DVE", "PL": "PL"}
for q, n in QSLOTS.items():
    for i in range(n):
        DOM2ENG[f"{q}{i}"] = QENG[q]
ENG2BASS = {"PE": "tensor", "ACT": "scalar", "DVE": "vector", "PL": "gpsimd",
            "SP": "sync"}


class Buf:
    __slots__ = ("name", "w", "r")

    def __init__(self, name):
        self.name = name
        self.w = None
        self.r = {}


class Builder:
    def __init__(self, nc):
        self.nc = nc
        self.sem = {d: nc.alloc_semaphore(f"sm_{d}") for d in DOMS}
        self.cnt = {d: 0 for d in DOMS}
        self.prog = {e: [] for e in ENG2BASS}
        self.know = {e: {d: 0 for d in DOMS} for e in ENG2BASS}
        self.snapc = {d: [] for d in DOMS}
        self.snapk = {d: [] for d in DOMS}

    def _wait(self, eng, dom, val):
        if val <= 0 or val <= self.know[eng].get(dom, 0):
            return
        self.prog[eng].append(("w", dom, val))
        k = self.know[eng]
        k[dom] = max(k[dom], val)
        i = bisect.bisect_left(self.snapc[dom], val)
        if i < len(self.snapc[dom]):
            for d2, v2 in self.snapk[dom][i].items():
                if v2 > k[d2]:
                    k[d2] = v2

    def op(self, dom, emit, reads=(), writes=(), inc=1):
        eng = DOM2ENG[dom]
        deps = {}
        for b in reads:
            t = b.w
            if t is not None and t[1] > deps.get(t[0], 0):
                deps[t[0]] = t[1]
        for b in writes:
            t = b.w
            if t is not None and t[1] > deps.get(t[0], 0):
                deps[t[0]] = t[1]
            for d2, v in b.r.items():
                if v > deps.get(d2, 0):
                    deps[d2] = v
        for d2 in sorted(deps):
            if eng == "PE" and DOM2ENG[d2] == "PE":
                continue
            self._wait(eng, d2, deps[d2])
        self.prog[eng].append(("op", emit))
        self.cnt[dom] += inc
        c = self.cnt[dom]
        tok = (dom, c)
        for b in reads:
            if b.r.get(dom, 0) < c:
                b.r[dom] = c
        for b in writes:
            b.w = tok
            b.r = {}
        self.snapc[dom].append(c)
        self.snapk[dom].append(dict(self.know[eng]))
        return tok

    def finish(self):
        for eng in ENG2BASS:
            for d in DOMS:
                if self.cnt[d] > 0:
                    self._wait(eng, d, self.cnt[d])

    def realize(self, block):
        sem = self.sem
        for eng, seq in self.prog.items():
            sec = getattr(block, ENG2BASS[eng])

            def mk(seq=seq):
                def body(e):
                    for it in seq:
                        if it[0] == "w":
                            e.wait_ge(sem[it[1]], it[2])
                        else:
                            it[1](e)
                return body
            sec(mk())


def make_regions():
    regs = []
    off = [0]

    def add(key, kc, cols):
        r = dict(key=key, kc=kc, cols=cols, off=off[0])
        off[0] += kc * cols
        regs.append(r)
        return r
    add(("wg2",), 4, 2048)
    add(("wg1",), 16, 512)
    for bk in range(4):
        add(("wu", bk), 16, 512)
    for bk in range(4):
        add(("wv", bk), 16, 512)
    for g in range(NGRP):
        for si, w in ((0, 512), (1, 512), (2, 384)):
            add(("gate", g, si), 16, w)
        for si, w in ((0, 512), (1, 512), (2, 384)):
            add(("up", g, si), 16, w)
        for si in range(8):
            add(("dn", g, si), IGC, 256)
    return regs, off[0]


def pack_host(inp, regs, total):
    wu_t = np.ascontiguousarray(inp["L0_attn_Wu"].T)
    wv_t = np.ascontiguousarray(inp["L0_attn_Wv"].T)
    wg1_t = np.ascontiguousarray(inp["L0_attn_Wg1"].T)
    wg2_t = np.ascontiguousarray(inp["L0_attn_Wg2"].T)
    sgu = inp["L0_ffn_Ws_gate_up"]
    sdn = inp["L0_ffn_Ws_down"]
    egu = inp["L0_ffn_We_gate_up"]
    edn = inp["L0_ffn_We_down"]

    def pm(mat, kc, c0, c1):
        return np.ascontiguousarray(
            mat.reshape(kc, 128, mat.shape[1])[:, :, c0:c1].transpose(1, 0, 2)
        ).reshape(128, kc * (c1 - c0))

    wflat = np.empty((128, total), np.float32)
    for r in regs:
        k = r["key"]
        kc, cols, off = r["kc"], r["cols"], r["off"]
        if k[0] == "wg2":
            piece = pm(wg2_t, 4, 0, 2048)
        elif k[0] == "wg1":
            piece = pm(wg1_t, 16, 0, 512)
        elif k[0] == "wu":
            piece = pm(wu_t, 16, k[1] * 512, (k[1] + 1) * 512)
        elif k[0] == "wv":
            piece = pm(wv_t, 16, k[1] * 512, (k[1] + 1) * 512)
        elif k[0] in ("gate", "up"):
            g, si = k[1], k[2]
            c0 = si * 512
            c1 = c0 + cols
            if g < 4:
                base = g * IG + (0 if k[0] == "gate" else 5632)
                piece = pm(sgu, 16, base + c0, base + c1)
            else:
                base = 0 if k[0] == "gate" else IG
                piece = pm(egu[g - 4], 16, base + c0, base + c1)
        else:
            g, si = k[1], k[2]
            c0, c1 = si * 256, si * 256 + 256
            if g < 4:
                piece = pm(sdn[g * IG:(g + 1) * IG], IGC, c0, c1)
            else:
                piece = pm(edn[g - 4], IGC, c0, c1)
        wflat[:, off:off + kc * cols] = piece
    return wflat


def build_program():
    nc = bass.Bass("TRN2", target_bir_lowering=False,
                   detect_race_conditions=False)
    regs, TOTW = make_regions()
    rmap = {r["key"]: r for r in regs}

    xt = nc.dram_tensor("xt", [D, L], F32, kind="ExternalInput").ap()
    wflat = nc.dram_tensor("wflat", [128, TOTW], F32, kind="ExternalInput").ap()
    bu16 = nc.dram_tensor("bu16", [128, 16], F32, kind="ExternalInput").ap()
    bv16 = nc.dram_tensor("bv16", [128, 16], F32, kind="ExternalInput").ap()
    bg1d = nc.dram_tensor("bg1d", [128, 4], F32, kind="ExternalInput").ap()
    bg2d = nc.dram_tensor("bg2d", [128, 16], F32, kind="ExternalInput").ap()
    cbd = nc.dram_tensor("cbd", [128, 16], F32, kind="ExternalInput").ap()
    cwd = nc.dram_tensor("cwd", [128, 48], F32, kind="ExternalInput").ap()
    wr9 = nc.dram_tensor("wr9", [128, 144], F32, kind="ExternalInput").ap()
    out_t = nc.dram_tensor("out_t", [D, L], F32, kind="ExternalOutput").ap()
    cflat = nc.dram_tensor("cflat", [128, TOTW], BF16, kind="Internal").ap()
    lnrow = nc.dram_tensor("lnrow", [2, LSH], F32, kind="Internal").ap()
    crow = nc.dram_tensor("crow", [9, L], BF16, kind="Internal").ap()

    sb = nc.alloc_sbuf_tensor
    acc = sb("acc", [128, NCH * LSH], F32)
    lnb = sb("lnb", [128, NCH * LSH], BF16)
    xbuf = sb("xbuf", [128, NCH * LS], BF16)
    hbuf = sb("hbuf", [128, 4 * LS], BF16)
    Abuf = [sb(f"A{i}", [128, IGC * LS], BF16) for i in range(2)]
    Wsl = [sb(f"W{i}", [128, 8192], BF16) for i in range(4)]
    Dsl = [sb(f"Dn{i}", [128, IGC * 256], BF16) for i in range(2)]
    kcb = [sb(f"kc{i}", [128, LS], F32) for i in range(2)]
    ugb = [sb(f"ug{i}", [128, LS], BF16) for i in range(2)]
    vgb = [sb(f"vg{i}", [128, LS], BF16) for i in range(2)]
    gtb = [sb(f"gt{i}", [128, LS], BF16) for i in range(2)]
    tmpb = [sb(f"tp{i}", [128, LSH], F32) for i in range(2)]
    xsqb = [sb(f"xq{i}", [128, LSH], BF16) for i in range(2)]
    abc = sb("abc", [128, LSH], F32)
    bbc = sb("bbc", [128, LSH], F32)
    cbc = [sb(f"cb{i}", [128, LS], BF16) for i in range(2)]
    sgbc = sb("sgbc", [128, LS], BF16)
    rows = sb("rows", [1, 4 * LSH], F32)
    smx = [sb(f"sx{i}", [128, 8], F32) for i in range(2)]
    msc = [sb(f"ms{i}", [128, 8], F32) for i in range(2)]
    red = [sb(f"rd{i}", [128, 8], F32) for i in range(2)]
    cmb = [sb(f"cm{i}", [128, 9], BF16) for i in range(2)]
    rlog = [sb(f"rl{i}", [128, 12], F32) for i in range(4)]
    bub = sb("bub", [128, 16], F32)
    bvb = sb("bvb", [128, 16], F32)
    bg1b = sb("bg1b", [128, 4], F32)
    bg2b = sb("bg2b", [128, 16], F32)
    cbb = sb("cbb", [128, 16], F32)
    cwb = sb("cwb", [128, 48], F32)
    wrb = sb("wrb", [128, 144], BF16)
    ones = sb("ones", [128, 2], BF16)
    epsb = sb("epsb", [1, 2], F32)
    onesf = sb("onesf", [1, 128], F32)
    stg = [sb(f"stg{i}", [128, 512], F32) for i in range(4)]
    carry = sb("carry", [128, 16], F32)

    psm = [nc.alloc_psum_tensor(f"pm{i}", [128, 512], F32) for i in range(4)]
    psa = [nc.alloc_psum_tensor(f"pa{i}", [128, 512], F32) for i in range(2)]
    pst = nc.alloc_psum_tensor("pst", [1, 1024], F32)

    b = Builder(nc)
    SEM = b.sem

    accB = [Buf(f"acc{c}") for c in range(NCH)]
    lnB = [Buf(f"ln{c}") for c in range(NCH)]
    xbB = [Buf(f"xb{c}") for c in range(NCH)]
    hB = Buf("h")
    AB = [Buf(f"A{i}") for i in range(2)]
    WB = [Buf(f"W{i}") for i in range(4)]
    DB = [Buf(f"D{i}") for i in range(2)]
    kcB = [Buf(f"kc{i}") for i in range(2)]
    ugB = [Buf(f"ug{i}") for i in range(2)]
    vgB = [Buf(f"vg{i}") for i in range(2)]
    gtB = [Buf(f"gt{i}") for i in range(2)]
    tmpB = [Buf(f"tp{i}") for i in range(2)]
    xsqB = [Buf(f"xq{i}") for i in range(2)]
    abcB, bbcB = Buf("abc"), Buf("bbc")
    cbcB = [Buf(f"cb{i}") for i in range(2)]
    sgbcB = Buf("sgbc")
    rowsB = Buf("rows")
    smxB = [Buf(f"sx{i}") for i in range(2)]
    mscB = [Buf(f"ms{i}") for i in range(2)]
    redB = [Buf(f"rd{i}") for i in range(2)]
    cmbB = [Buf(f"cm{i}") for i in range(2)]
    rlogB = [Buf(f"rl{i}") for i in range(4)]
    constB = Buf("consts")
    stgB = [Buf(f"stg{i}") for i in range(4)]
    wrB = Buf("wrb")
    carryB = [Buf(f"cy{c}") for c in range(NCH)]
    psmB = [Buf(f"pm{i}") for i in range(4)]
    psaB = [Buf(f"pa{i}") for i in range(2)]
    pstB = Buf("pst")
    cacheB = {r["key"]: Buf(str(r["key"])) for r in regs}
    lnrowB = Buf("lnrow")
    crowB = Buf("crow")
    outB = Buf("out_t")

    psm_rot = [0]

    def next_psm():
        i = psm_rot[0] % 4
        psm_rot[0] += 1
        return i

    def acc_full(c):
        return acc[:, c * LSH + 1:c * LSH + 515]

    def acc_own(c):
        return acc[:, c * LSH + 2:c * LSH + 2 + LS]

    def acc_chunk(c):
        return acc[:, c * LSH:(c + 1) * LSH]

    def ln_full(c):
        return lnb[:, c * LSH + 1:c * LSH + 515]

    def ln_chunk(c):
        return lnb[:, c * LSH:(c + 1) * LSH]

    def ln_own(c):
        return lnb[:, c * LSH + 2:c * LSH + 2 + LS]

    def xb_ch(c):
        return xbuf[:, c * LS:(c + 1) * LS]

    def run_mms(e, mms):
        ins = None
        for (o, lt, rh, st, sp) in mms:
            ins = e.matmul(o, lt, rh, start=st, stop=sp)
        ins.then_inc(SEM["PE"], 1)

    def peg(psb, mms, reads):
        b.op("PE", (lambda mms=tuple(mms): (lambda e: run_mms(e, mms)))(),
             reads=list(reads), writes=[psb])

    def dve(emit1, reads, writes):
        def emit(e, f=emit1):
            f(e).then_inc(SEM["DVE"], 1)
        return b.op("DVE", emit, reads=list(reads), writes=list(writes))

    def act(emit1, reads, writes):
        def emit(e, f=emit1):
            f(e).then_inc(SEM["ACT"], 1)
        return b.op("ACT", emit, reads=list(reads), writes=list(writes))

    qrot = {q: 0 for q in QSLOTS}
    quseB = {q: [Buf(f"{q}use{i}") for i in range(n)] for q, n in QSLOTS.items()}

    def dma(q, pairs, reads, writes, **kw):
        k = qrot[q] % QSLOTS[q]
        qrot[q] += 1
        dom = f"{q}{k}"

        def emit(e, pairs=tuple(pairs), kw=kw, dom=dom):
            for (dst, src) in pairs:
                e.dma_start(dst, src, **kw).then_inc(SEM[dom], 16)
        return b.op(dom, emit, reads=list(reads),
                    writes=list(writes) + [quseB[q][k]],
                    inc=16 * len(pairs))

    cast_rot = [0]

    def cast_piece(ceng, dst_ap, src_ap, reads, writes):
        if ceng == "DVE":
            dve(lambda e: e.tensor_copy(dst_ap, src_ap), reads, writes)
        elif ceng == "ACT":
            act(lambda e: e.copy(dst_ap, src_ap), reads, writes)
        else:
            def emit(e):
                e.tensor_copy(dst_ap, src_ap).then_inc(SEM["PL"], 1)
            b.op("PL", emit, reads=list(reads), writes=list(writes))

    wbrot = [0]

    def wqueue(key):
        return "SP"

    def load_region(seg, key, slot_i, dslot=False):
        r = rmap[key]
        w = r["kc"] * r["cols"]
        off = r["off"]
        slot = (Dsl if dslot else Wsl)[slot_i]
        dstap = slot[:, 0:w]
        slotB = (DB if dslot else WB)[slot_i]
        cB = cacheB[key]
        q = wqueue(key)
        if seg == 0:
            ceng = ("DVE", "ACT", "PL")[cast_rot[0] % 3]
            cast_rot[0] += 1
            poff = 0
            qbase = 0 if q == "SP" else 2
            while poff < w:
                pw = min(512, w - poff)
                pp = qbase + (poff // 512) % 2
                dma(q, [(stg[pp][:, 0:pw],
                         wflat[:, off + poff:off + poff + pw])],
                    reads=[], writes=[stgB[pp]])
                cast_piece(ceng, slot[:, poff:poff + pw], stg[pp][:, 0:pw],
                           [stgB[pp]], [slotB])
                poff += pw
            wbq = "ACTQ" if (wbrot[0] % 2) else "SP"
            wbrot[0] += 1
            dma(wbq, [(cflat[:, off:off + w], dstap)],
                reads=[slotB], writes=[cB])
        else:
            dma(q, [(dstap, cflat[:, off:off + w])], reads=[cB], writes=[slotB])
        return r

    def lhsT(slot_i, r, k, j, dslot=False):
        slot = (Dsl if dslot else Wsl)[slot_i]
        base = k * r["cols"] + j * 128
        return slot[:, base:base + 128]

    def emit_pro(e):
        e.memset(carry[:, :], 0.0)
        e.memset(epsb[:, :], EPS)
        e.memset(onesf[:, :], 1.0)
        e.memset(ones[:, :], 1.0).then_inc(SEM["DVE"], 4)
    tok = b.op("DVE", emit_pro, reads=[], writes=[constB], inc=4)
    for cc in carryB:
        cc.w = tok
    dma("ACTQ", [(bub[:, :], bu16), (bvb[:, :], bv16), (bg1b[:, :], bg1d),
                 (bg2b[:, :], bg2d), (cbb[:, :], cbd), (cwb[:, :], cwd)],
        reads=[], writes=[constB])
    dma("SP", [(stg[0][:, 0:144], wr9)], reads=[], writes=[stgB[0]])
    dve(lambda e: e.tensor_copy(wrb[:, :], stg[0][:, 0:144]),
        reads=[stgB[0]], writes=[wrB])

    def peg_stats(mms, reads):
        b.op("PE", (lambda mms=tuple(mms): (lambda e: run_mms(e, mms)))(),
             reads=list(reads), writes=[pstB])

    def ln_cast(which, c):
        src = acc_full if which == 1 else acc_own
        dstf = ln_full if which == 1 else ln_own
        dve(lambda e, c=c: e.tensor_copy(dstf(c), src(c)),
            reads=[accB[c]], writes=[lnB[c]])

    def layernorm(which, skip_casts=False):
        n = 514 if which == 1 else LS
        src = acc_full if which == 1 else acc_own
        dstf = ln_full if which == 1 else ln_own
        if not skip_casts:
            for c in range(NCH):
                ln_cast(which, c)
        for c in range(NCH):
            mms = [(pst[0:1, 0:512], ones[:, 0:1], dstf(c)[:, 0:512],
                    c == 0, c == NCH - 1)]
            if which == 1:
                mms.append((pst[0:1, 512:514], ones[:, 0:1],
                            dstf(c)[:, 512:514], c == 0, c == NCH - 1))
            peg_stats(mms, [lnB[c], constB])
        act(lambda e: e.copy(rows[:, 0:n], pst[0:1, 0:n]),
            reads=[pstB], writes=[rowsB])
        for c in range(NCH):
            p = c % 2
            act(lambda e, c=c, p=p: e.activation(
                xsqb[p][:, 0:n], src(c), AF.Square, bias=0.0, scale=1.0),
                reads=[accB[c]], writes=[xsqB[p]])
            mms = [(pst[0:1, 0:512], ones[:, 0:1], xsqb[p][:, 0:512],
                    c == 0, c == NCH - 1)]
            if which == 1:
                mms.append((pst[0:1, 512:514], ones[:, 0:1],
                            xsqb[p][:, 512:514], c == 0, c == NCH - 1))
            peg_stats(mms, [xsqB[p], constB])
        act(lambda e: e.copy(rows[:, LSH:LSH + n], pst[0:1, 0:n]),
            reads=[pstB], writes=[rowsB])
        S = rows[:, 0:n]
        Q = rows[:, LSH:LSH + n]
        A_ = rows[:, 2 * LSH:2 * LSH + n]
        B_ = rows[:, 3 * LSH:3 * LSH + n]
        dve(lambda e: e.tensor_mul(A_, S, S), reads=[rowsB], writes=[rowsB])
        dve(lambda e: e.scalar_tensor_tensor(Q, A_, -1.0 / D, Q, AL.mult, AL.add),
            reads=[rowsB], writes=[rowsB])
        act(lambda e: e.activation(A_, Q, AF.Ln, bias=epsb[:, 0:1], scale=1.0 / D),
            reads=[rowsB, constB], writes=[rowsB])
        act(lambda e: e.activation(Q, A_, AF.Exp, bias=0.0, scale=-0.5),
            reads=[rowsB], writes=[rowsB])
        dve(lambda e: e.scalar_tensor_tensor(B_, S, -1.0 / D, Q, AL.mult, AL.mult),
            reads=[rowsB], writes=[rowsB])
        for row_ap, dstt, dstB in ((Q, abc, abcB), (B_, bbc, bbcB)):
            mm1 = [(psa[0][:, 0:512], onesf[0:1, :], row_ap[0:1, 0:512],
                    True, True)]
            b.op("PE", (lambda mms=tuple(mm1): (lambda e: run_mms(e, mms)))(),
                 reads=[rowsB, constB], writes=[psaB[0]])
            act(lambda e, dstt=dstt: e.copy(dstt[:, 0:512], psa[0][:, 0:512]),
                reads=[psaB[0]], writes=[dstB])
            if n > 512:
                mm2 = [(psa[1][:, 0:n - 512], onesf[0:1, :],
                        row_ap[0:1, 512:n], True, True)]
                b.op("PE", (lambda mms=tuple(mm2): (lambda e: run_mms(e, mms)))(),
                     reads=[rowsB, constB], writes=[psaB[1]])
                act(lambda e, dstt=dstt, n=n: e.copy(dstt[:, 512:n],
                                                     psa[1][:, 0:n - 512]),
                    reads=[psaB[1]], writes=[dstB])
        for c in range(NCH):
            p = c % 2
            dve(lambda e, c=c, p=p: e.tensor_mul(
                tmpb[p][:, 0:n], src(c), abc[:, 0:n]),
                reads=[accB[c], abcB], writes=[tmpB[p]])
            dve(lambda e, c=c, p=p: e.tensor_add(
                dstf(c), tmpb[p][:, 0:n], bbc[:, 0:n]),
                reads=[tmpB[p], bbcB], writes=[lnB[c]])

    def dn_load(seg, g, si):
        return load_region(seg, ("dn", g, si), si % 2, dslot=True)

    def emit_down(seg, g, smap):
        gpar = g % 2
        for dc in range(NCH):
            si = dc // 2
            dsl_i, rdn = smap[("dn", si)]
            pa_i = dc % 2
            mms = [(psa[pa_i][:, :], lhsT(dsl_i, rdn, k, dc % 2, dslot=True),
                    Abuf[gpar][:, k * LS:(k + 1) * LS],
                    k == 0, k == IGC - 1) for k in range(IGC)]
            b.op("PE", (lambda mms=tuple(mms): (lambda e: run_mms(e, mms)))(),
                 reads=[DB[dsl_i], AB[gpar]], writes=[psaB[pa_i]])
            dve(lambda e, dc=dc, pa_i=pa_i: e.tensor_add(
                acc_own(dc), acc_own(dc), psa[pa_i][:, :]),
                reads=[psaB[pa_i]], writes=[accB[dc]])
            if dc % 2 == 1 and si + 2 <= 7:
                smap[("dn", si + 2)] = ((si + 2) % 2, dn_load(seg, g, si + 2))


    def emit_softmax(p, t, nm, sm, rs, m1, m2, cpairs):
        rl = rlog[t]
        dve(lambda e: e.reduce_max(nm, rl[:, 0:8], axis=AX, negate=True),
            reads=[rlogB[t]], writes=[redB[p]])
        act(lambda e: e.activation(smx[p][:, :], rl[:, 0:8], AF.Exp, bias=nm,
                                   scale=1.0, accum_out=sm),
            reads=[rlogB[t], redB[p]], writes=[smxB[p], redB[p]])
        act(lambda e: e.activation(cmb[p][:, 8:9], rl[:, 8:9], AF.Sigmoid,
                                   bias=0.0, scale=1.0),
            reads=[rlogB[t]], writes=[cmbB[p]])
        dve(lambda e: e.reciprocal(rs, sm), reads=[redB[p]], writes=[redB[p]])
        dve(lambda e: e.tensor_scalar(smx[p][:, :], smx[p][:, :], rs, None,
                                      AL.mult),
            reads=[smxB[p], redB[p]], writes=[smxB[p]])
        dve(lambda e: e.reduce_max(m1, smx[p][:, :], axis=AX),
            reads=[smxB[p]], writes=[redB[p]])
        dve(lambda e: e.tensor_scalar(msc[p][:, :], smx[p][:, :], m1, None,
                                      AL.is_lt),
            reads=[smxB[p], redB[p]], writes=[mscB[p]])
        dve(lambda e: e.tensor_mul(msc[p][:, :], smx[p][:, :], msc[p][:, :]),
            reads=[smxB[p], mscB[p]], writes=[mscB[p]])
        dve(lambda e: e.reduce_max(m2, msc[p][:, :], axis=AX),
            reads=[mscB[p]], writes=[redB[p]])
        dve(lambda e: e.tensor_scalar(msc[p][:, :], smx[p][:, :], m2, None,
                                      AL.is_ge),
            reads=[smxB[p], redB[p]], writes=[mscB[p]])
        dve(lambda e: e.tensor_mul(cmb[p][:, 0:8], smx[p][:, :], msc[p][:, :]),
            reads=[smxB[p], mscB[p]], writes=[cmbB[p]])
        dma("ROWQ", cpairs, reads=[cmbB[p]], writes=[crowB],
            allow_slow_non_contiguous=True)

    for seg in range(NSEG):
        l0 = seg * LS
        pairs = []
        for c in range(NCH):
            sa = max(l0 - 1, 0)
            sbnd = min(l0 + LS + 1, L)
            doff = 2 - (l0 - sa)
            pairs.append((acc[:, c * LSH + doff:c * LSH + doff + (sbnd - sa)],
                          xt[c * 128:(c + 1) * 128, sa:sbnd]))
        for gq in range(4):
            dma("ACTQ", pairs[gq * 4:(gq + 1) * 4], reads=[],
                writes=accB[gq * 4:(gq + 1) * 4])
        if seg == 0:
            dve(lambda e: e.memset(acc[:, 1:NCH * LSH:LSH], 0.0),
                reads=[], writes=accB)
        if seg == NSEG - 1:
            dve(lambda e: e.memset(acc[:, 514:NCH * LSH:LSH], 0.0),
                reads=[], writes=accB)

        layernorm(1)

        acyc = [0]

        def aslot():
            i = acyc[0] % 3
            acyc[0] += 1
            return i
        rg2 = load_region(seg, ("wg2",), 3)
        rg1 = load_region(seg, ("wg1",), aslot())

        for c in range(NCH):
            p = c % 2
            w0 = cwb[:, 3 * c:3 * c + 1]
            w1 = cwb[:, 3 * c + 1:3 * c + 2]
            w2 = cwb[:, 3 * c + 2:3 * c + 3]
            cb_ = cbb[:, c:c + 1]
            lf = ln_chunk(c)
            dve(lambda e, p=p, w0=w0, cb_=cb_, lf=lf: e.tensor_scalar(
                tmpb[p][:, 0:LS], lf[:, 1:513], w0, cb_, AL.mult, AL.add),
                reads=[lnB[c], constB], writes=[tmpB[p]])
            dve(lambda e, p=p, w1=w1, lf=lf: e.scalar_tensor_tensor(
                tmpb[p][:, 0:LS], lf[:, 2:514], w1, tmpb[p][:, 0:LS],
                AL.mult, AL.add),
                reads=[lnB[c], constB, tmpB[p]], writes=[tmpB[p]])
            dve(lambda e, p=p, w2=w2, lf=lf: e.scalar_tensor_tensor(
                tmpb[p][:, 0:LS], lf[:, 3:515], w2, tmpb[p][:, 0:LS],
                AL.mult, AL.add),
                reads=[lnB[c], constB, tmpB[p]], writes=[tmpB[p]])
            dve(lambda e, c=c, p=p: e.tensor_add(
                acc_own(c), acc_own(c), tmpb[p][:, 0:LS]),
                reads=[tmpB[p]], writes=[accB[c]])

        for hc in range(4):
            pi = next_psm()
            mms = [(psm[pi][:, :], lhsT(0, rg1, k, hc), ln_own(k),
                    k == 0, k == NCH - 1) for k in range(NCH)]
            peg(psmB[pi], mms, [WB[0]] + lnB)
            act(lambda e, hc=hc, pi=pi: e.activation(
                hbuf[:, hc * LS:(hc + 1) * LS], psm[pi][:, :], AF.Relu,
                bias=bg1b[:, hc:hc + 1], scale=1.0),
                reads=[psmB[pi], constB], writes=[hB])

        cur = {}
        for rc in range(NCH):
            blk = rc // 4
            if rc % 4 == 0:
                su = aslot()
                ru = load_region(seg, ("wu", blk), su)
                sv = aslot()
                rv = load_region(seg, ("wv", blk), sv)
                cur[blk] = (su, ru, sv, rv)
            su, ru, sv, rv = cur[blk]
            p = rc % 2
            pg = next_psm()
            mms = [(psm[pg][:, :], lhsT(3, rg2, hk, rc),
                    hbuf[:, hk * LS:(hk + 1) * LS], hk == 0, hk == 3)
                   for hk in range(4)]
            peg(psmB[pg], mms, [WB[3], hB])
            act(lambda e, rc=rc, pg=pg, p=p: e.activation(
                gtb[p][:, :], psm[pg][:, :], AF.Sigmoid,
                bias=bg2b[:, rc:rc + 1], scale=1.0),
                reads=[psmB[pg], constB], writes=[gtB[p]])
            pu = next_psm()
            mms = [(psm[pu][:, :], lhsT(su, ru, k, rc % 4), ln_own(k),
                    k == 0, k == NCH - 1) for k in range(NCH)]
            peg(psmB[pu], mms, [WB[su]] + lnB)
            dve(lambda e, rc=rc, pu=pu, p=p: e.scalar_tensor_tensor(
                ugb[p][:, :], psm[pu][:, :], bub[:, rc:rc + 1], gtb[p][:, :],
                AL.add, AL.mult),
                reads=[psmB[pu], constB, gtB[p]], writes=[ugB[p]])
            pv = next_psm()
            mms = [(psm[pv][:, :], lhsT(sv, rv, k, rc % 4), ln_own(k),
                    k == 0, k == NCH - 1) for k in range(NCH)]
            peg(psmB[pv], mms, [WB[sv]] + lnB)
            dve(lambda e, rc=rc, pv=pv, p=p: e.scalar_tensor_tensor(
                vgb[p][:, :], psm[pv][:, :], bvb[:, rc:rc + 1], gtb[p][:, :],
                AL.add, AL.mult),
                reads=[psmB[pv], constB, gtB[p]], writes=[vgB[p]])
            dve(lambda e, rc=rc, p=p: e.tensor_tensor_scan(
                kcb[p][:, :], ugb[p][:, :], ugb[p][:, :],
                carry[:, rc:rc + 1], AL.add, AL.bypass),
                reads=[ugB[p], carryB[rc]], writes=[kcB[p]])
            dve(lambda e, rc=rc, p=p: e.tensor_copy(
                carry[:, rc:rc + 1], kcb[p][:, LS - 1:LS]),
                reads=[kcB[p]], writes=[carryB[rc]])
            dve(lambda e, rc=rc, p=p: e.tensor_mul(
                xb_ch(rc), kcb[p][:, :], vgb[p][:, :]),
                reads=[kcB[p], vgB[p]], writes=[xbB[rc]])

        for dc in range(NCH):
            blk = dc // 4
            if dc % 4 == 0:
                si_uv = aslot()
                ruv = load_region(seg, ("wu", blk), si_uv)
                cur[("uv", blk)] = (si_uv, ruv)
            si_uv, ruv = cur[("uv", blk)]
            pa_i = dc % 2
            mms = [(psa[pa_i][:, :], lhsT(si_uv, ruv, k, dc % 4), xb_ch(k),
                    k == 0, k == NCH - 1) for k in range(NCH)]
            b.op("PE", (lambda mms=tuple(mms): (lambda e: run_mms(e, mms)))(),
                 reads=[WB[si_uv]] + xbB, writes=[psaB[pa_i]])
            dve(lambda e, dc=dc, pa_i=pa_i: e.tensor_add(
                acc_own(dc), acc_own(dc), psa[pa_i][:, :]),
                reads=[psaB[pa_i]], writes=[accB[dc]])
            ln_cast(2, dc)

        layernorm(2, skip_casts=True)

        router_tail = []
        for t in range(4):
            p = t % 2
            pr_i = next_psm()
            mms = [(psm[pr_i][:, 0:9],
                    lnb[:, k * LSH + 2 + t * 128:k * LSH + 2 + (t + 1) * 128],
                    wrb[:, k * 9:(k + 1) * 9], k == 0, k == NCH - 1)
                   for k in range(NCH)]
            peg(psmB[pr_i], mms, lnB + [wrB])
            act(lambda e, t=t, pr_i=pr_i: e.copy(rlog[t][:, 0:9],
                                                 psm[pr_i][:, 0:9]),
                reads=[psmB[pr_i]], writes=[rlogB[t]])
            nm = red[p][:, 0:1]
            sm = red[p][:, 1:2]
            rs = red[p][:, 2:3]
            m1 = red[p][:, 3:4]
            m2 = red[p][:, 4:5]
            cpairs = []
            for ei in range(9):
                dst = crow[ei:ei + 1, l0 + t * 128:l0 + (t + 1) * 128]
                cpairs.append((dst.rearrange("o l -> l o"),
                               cmb[p][:, ei:ei + 1]))
            router_tail.append(
                (lambda p=p, t=t, nm=nm, sm=sm, rs=rs, m1=m1, m2=m2,
                        cpairs=tuple(cpairs):
                 emit_softmax(p, t, nm, sm, rs, m1, m2, list(cpairs))))

        def emit_x2s():
            dma("ROWQ", [(sgbc[:, :],
                          crow[8:9, l0:l0 + LS].broadcast_to([128, LS]))],
                reads=[crowB], writes=[sgbcB])
            for c in range(NCH):
                def emit_xs(e, c=c):
                    e.tensor_mul(xb_ch(c), ln_own(c),
                                 sgbc[:, :]).then_inc(SEM["PL"], 1)
                b.op("PL", emit_xs, reads=[lnB[c], sgbcB], writes=[xbB[c]])

        fcyc = [0]

        def fslot():
            i = [3, 0, 1, 2][fcyc[0] % 4]
            fcyc[0] += 1
            return i
        gsmaps = {}
        for g in range(NGRP):
            smap = {}
            gsmaps[g] = smap
            if g >= 1:
                pm1 = gsmaps[g - 1]
                pm1[("dn", 0)] = (0, dn_load(seg, g - 1, 0))
                pm1[("dn", 1)] = (1, dn_load(seg, g - 1, 1))
            gpar = g % 2
            for ii in range(IGC):
                si = ii // 4
                if ii % 4 == 0:
                    sl = fslot()
                    smap[("gate", si)] = (sl, load_region(seg, ("gate", g, si), sl))
                sl, r_ = smap[("gate", si)]
                pg = next_psm()
                mms = [(psm[pg][:, :], lhsT(sl, r_, k, ii - 4 * si), ln_own(k),
                        k == 0, k == NCH - 1) for k in range(NCH)]
                peg(psmB[pg], mms, [WB[sl]] + lnB)
                act(lambda e, ii=ii, pg=pg, gpar=gpar: e.activation(
                    Abuf[gpar][:, ii * LS:(ii + 1) * LS], psm[pg][:, :],
                    AF.Silu, bias=0.0, scale=1.0),
                    reads=[psmB[pg]], writes=[AB[gpar]])
            if g == 0:
                for f in router_tail:
                    f()
                emit_x2s()
            for ii in range(IGC):
                si = ii // 4
                if ii % 4 == 0:
                    sl = fslot()
                    smap[("up", si)] = (sl, load_region(seg, ("up", g, si), sl))
                sl, r_ = smap[("up", si)]
                pc = next_psm()
                mms = [(psm[pc][:, :], lhsT(sl, r_, k, ii - 4 * si), xb_ch(k),
                        k == 0, k == NCH - 1) for k in range(NCH)]
                b.op("PE", (lambda mms=tuple(mms): (lambda e: run_mms(e, mms)))(),
                     reads=[WB[sl]] + xbB, writes=[psmB[pc]])
                dve(lambda e, ii=ii, pc=pc, gpar=gpar: e.tensor_mul(
                    Abuf[gpar][:, ii * LS:(ii + 1) * LS],
                    psm[pc][:, :], Abuf[gpar][:, ii * LS:(ii + 1) * LS]),
                    reads=[psmB[pc], AB[gpar]], writes=[AB[gpar]])
            if 4 <= g + 1 < NGRP:
                e_next = g + 1 - 4
                cp = e_next % 2
                dma("ROWQ", [(cbc[cp][:, :],
                              crow[e_next:e_next + 1, l0:l0 + LS]
                              .broadcast_to([128, LS]))],
                    reads=[crowB], writes=[cbcB[cp]])
                for c in range(NCH):
                    def emit_xc(e, c=c, cp=cp):
                        e.tensor_mul(xb_ch(c), ln_own(c), cbc[cp][:, :]).then_inc(SEM["PL"], 1)
                    b.op("PL", emit_xc, reads=[lnB[c], cbcB[cp]], writes=[xbB[c]])
            if g >= 1:
                emit_down(seg, g - 1, gsmaps[g - 1])
        gsmaps[NGRP - 1][("dn", 0)] = (0, dn_load(seg, NGRP - 1, 0))
        gsmaps[NGRP - 1][("dn", 1)] = (1, dn_load(seg, NGRP - 1, 1))
        emit_down(seg, NGRP - 1, gsmaps[NGRP - 1])

        spairs = [(out_t[c * 128:(c + 1) * 128, l0:l0 + LS], acc_own(c))
                  for c in range(NCH)]
        for gq in range(4):
            dma("ACTQ", spairs[gq * 4:(gq + 1) * 4],
                reads=accB[gq * 4:(gq + 1) * 4], writes=[outB])

    b.finish()
    with nc.Block() as block:
        b.realize(block)
    return nc


def kernel(**inputs):
    regs, TOTW = make_regions()
    wflat = pack_host(inputs, regs, TOTW)
    bu16 = np.ascontiguousarray(inputs["L0_attn_bu"].reshape(16, 128).T)
    bv16 = np.ascontiguousarray(inputs["L0_attn_bv"].reshape(16, 128).T)
    bg1d = np.ascontiguousarray(inputs["L0_attn_bg1"].reshape(4, 128).T)
    bg2d = np.ascontiguousarray(inputs["L0_attn_bg2"].reshape(16, 128).T)
    cbd = np.ascontiguousarray(inputs["L0_attn_conv_b"].reshape(16, 128).T)
    cwd = np.ascontiguousarray(
        inputs["L0_attn_conv_w"].reshape(16, 128, 3)
        .transpose(1, 0, 2).reshape(128, 48))
    wr9 = np.ascontiguousarray(
        np.concatenate([inputs["L0_ffn_W_router"], inputs["L0_ffn_W_sgate"]], 0)
        .T.reshape(16, 128, 9).transpose(1, 0, 2).reshape(128, 144))
    x = inputs["L0_attn_x"]

    nc = build_program()
    in_maps = []
    for c in range(8):
        in_maps.append(dict(
            xt=np.ascontiguousarray(x[c].T), wflat=wflat, bu16=bu16,
            bv16=bv16, bg1d=bg1d, bg2d=bg2d, cbd=cbd, cwd=cwd, wr9=wr9))
    res = run_bass_kernel_spmd(nc, in_maps, core_ids=list(range(8)))
    out = np.stack([np.ascontiguousarray(r["out_t"].T) for r in res.results])
    return out.astype(np.float32)


# version 05
# speedup vs baseline: 1.3731x; 1.3731x over previous
import bisect
import numpy as np
import concourse.bass as bass
import concourse.mybir as mybir
from concourse.bass_utils import run_bass_kernel_spmd

F32 = mybir.dt.float32
BF16 = mybir.dt.bfloat16
AL = mybir.AluOpType
AF = mybir.ActivationFunctionType
AX = mybir.AxisListType.X

D = 2048
L = 4096
NCH = 16
LS = 512
NSEG = L // LS
LSH = LS + 4
IG = 1408
IGC = 11
NGRP = 12
EPS = 1e-5

QSLOTS = {"SP": 12, "ACTQ": 12, "ROWQ": 4}
QENG = {"SP": "SP", "ACTQ": "ACT", "ROWQ": "ACT"}
DOMS = ["PE", "ACT", "DVE", "PL"] + [f"{q}{i}" for q, n in QSLOTS.items()
                                     for i in range(n)]
DOM2ENG = {"PE": "PE", "ACT": "ACT", "DVE": "DVE", "PL": "PL"}
for q, n in QSLOTS.items():
    for i in range(n):
        DOM2ENG[f"{q}{i}"] = QENG[q]
ENG2BASS = {"PE": "tensor", "ACT": "scalar", "DVE": "vector", "PL": "gpsimd",
            "SP": "sync"}


class Buf:
    __slots__ = ("name", "w", "r")

    def __init__(self, name):
        self.name = name
        self.w = None
        self.r = {}


class Builder:
    def __init__(self, nc):
        self.nc = nc
        self.sem = {d: nc.alloc_semaphore(f"sm_{d}") for d in DOMS}
        self.cnt = {d: 0 for d in DOMS}
        self.prog = {e: [] for e in ENG2BASS}
        self.know = {e: {d: 0 for d in DOMS} for e in ENG2BASS}
        self.snapc = {d: [] for d in DOMS}
        self.snapk = {d: [] for d in DOMS}

    def _wait(self, eng, dom, val):
        if val <= 0 or val <= self.know[eng].get(dom, 0):
            return
        self.prog[eng].append(("w", dom, val))
        k = self.know[eng]
        k[dom] = max(k[dom], val)
        i = bisect.bisect_left(self.snapc[dom], val)
        if i < len(self.snapc[dom]):
            for d2, v2 in self.snapk[dom][i].items():
                if v2 > k[d2]:
                    k[d2] = v2

    def op(self, dom, emit, reads=(), writes=(), inc=1):
        eng = DOM2ENG[dom]
        deps = {}
        for b in reads:
            t = b.w
            if t is not None and t[1] > deps.get(t[0], 0):
                deps[t[0]] = t[1]
        for b in writes:
            t = b.w
            if t is not None and t[1] > deps.get(t[0], 0):
                deps[t[0]] = t[1]
            for d2, v in b.r.items():
                if v > deps.get(d2, 0):
                    deps[d2] = v
        for d2 in sorted(deps):
            if eng == "PE" and DOM2ENG[d2] == "PE":
                continue
            self._wait(eng, d2, deps[d2])
        self.prog[eng].append(("op", emit))
        self.cnt[dom] += inc
        c = self.cnt[dom]
        tok = (dom, c)
        for b in reads:
            if b.r.get(dom, 0) < c:
                b.r[dom] = c
        for b in writes:
            b.w = tok
            b.r = {}
        self.snapc[dom].append(c)
        self.snapk[dom].append(dict(self.know[eng]))
        return tok

    def finish(self):
        for eng in ENG2BASS:
            for d in DOMS:
                if self.cnt[d] > 0:
                    self._wait(eng, d, self.cnt[d])

    def realize(self, block):
        sem = self.sem
        for eng, seq in self.prog.items():
            sec = getattr(block, ENG2BASS[eng])

            def mk(seq=seq):
                def body(e):
                    for it in seq:
                        if it[0] == "w":
                            e.wait_ge(sem[it[1]], it[2])
                        else:
                            it[1](e)
                return body
            sec(mk())


def make_regions():
    regs = []
    off = [0]

    def add(key, kc, cols):
        r = dict(key=key, kc=kc, cols=cols, off=off[0])
        off[0] += kc * cols
        regs.append(r)
        return r
    add(("wg2",), 4, 2048)
    add(("wg1",), 16, 512)
    for bk in range(4):
        add(("wu", bk), 16, 512)
    for bk in range(4):
        add(("wv", bk), 16, 512)
    for g in range(NGRP):
        for si, w in ((0, 512), (1, 512), (2, 384)):
            add(("gate", g, si), 16, w)
        for si, w in ((0, 512), (1, 512), (2, 384)):
            add(("up", g, si), 16, w)
        for si in range(8):
            add(("dn", g, si), IGC, 256)
    return regs, off[0]


def pack_host(inp, regs, total):
    wu_t = np.ascontiguousarray(inp["L0_attn_Wu"].T)
    wv_t = np.ascontiguousarray(inp["L0_attn_Wv"].T)
    wg1_t = np.ascontiguousarray(inp["L0_attn_Wg1"].T)
    wg2_t = np.ascontiguousarray(inp["L0_attn_Wg2"].T)
    sgu = inp["L0_ffn_Ws_gate_up"]
    sdn = inp["L0_ffn_Ws_down"]
    egu = inp["L0_ffn_We_gate_up"]
    edn = inp["L0_ffn_We_down"]

    def pm(mat, kc, c0, c1):
        return np.ascontiguousarray(
            mat.reshape(kc, 128, mat.shape[1])[:, :, c0:c1].transpose(1, 0, 2)
        ).reshape(128, kc * (c1 - c0))

    wflat = np.empty((128, total), np.float32)
    for r in regs:
        k = r["key"]
        kc, cols, off = r["kc"], r["cols"], r["off"]
        if k[0] == "wg2":
            piece = pm(wg2_t, 4, 0, 2048)
        elif k[0] == "wg1":
            piece = pm(wg1_t, 16, 0, 512)
        elif k[0] == "wu":
            piece = pm(wu_t, 16, k[1] * 512, (k[1] + 1) * 512)
        elif k[0] == "wv":
            piece = pm(wv_t, 16, k[1] * 512, (k[1] + 1) * 512)
        elif k[0] in ("gate", "up"):
            g, si = k[1], k[2]
            c0 = si * 512
            c1 = c0 + cols
            if g < 4:
                base = g * IG + (0 if k[0] == "gate" else 5632)
                piece = pm(sgu, 16, base + c0, base + c1)
            else:
                base = 0 if k[0] == "gate" else IG
                piece = pm(egu[g - 4], 16, base + c0, base + c1)
        else:
            g, si = k[1], k[2]
            c0, c1 = si * 256, si * 256 + 256
            if g < 4:
                piece = pm(sdn[g * IG:(g + 1) * IG], IGC, c0, c1)
            else:
                piece = pm(edn[g - 4], IGC, c0, c1)
        wflat[:, off:off + kc * cols] = piece
    return wflat


def build_program():
    nc = bass.Bass("TRN2", target_bir_lowering=False,
                   detect_race_conditions=False)
    regs, TOTW = make_regions()
    rmap = {r["key"]: r for r in regs}

    xt = nc.dram_tensor("xt", [D, L], F32, kind="ExternalInput").ap()
    wflat = nc.dram_tensor("wflat", [128, TOTW], F32, kind="ExternalInput").ap()
    bu16 = nc.dram_tensor("bu16", [128, 16], F32, kind="ExternalInput").ap()
    bv16 = nc.dram_tensor("bv16", [128, 16], F32, kind="ExternalInput").ap()
    bg1d = nc.dram_tensor("bg1d", [128, 4], F32, kind="ExternalInput").ap()
    bg2d = nc.dram_tensor("bg2d", [128, 16], F32, kind="ExternalInput").ap()
    cbd = nc.dram_tensor("cbd", [128, 16], F32, kind="ExternalInput").ap()
    cwd = nc.dram_tensor("cwd", [128, 48], F32, kind="ExternalInput").ap()
    wr9 = nc.dram_tensor("wr9", [128, 144], F32, kind="ExternalInput").ap()
    identd = nc.dram_tensor("identd", [128, 128], F32, kind="ExternalInput").ap()
    out_t = nc.dram_tensor("out_t", [D, L], F32, kind="ExternalOutput").ap()
    cflat = nc.dram_tensor("cflat", [128, TOTW], BF16, kind="Internal").ap()
    lnrow = nc.dram_tensor("lnrow", [2, LSH], F32, kind="Internal").ap()
    crow = nc.dram_tensor("crow", [9, L], BF16, kind="Internal").ap()

    sb = nc.alloc_sbuf_tensor
    acc = sb("acc", [128, NCH * LSH], F32)
    lnb = sb("lnb", [128, NCH * LSH], BF16)
    xbuf = sb("xbuf", [128, NCH * LS], BF16)
    hbuf = sb("hbuf", [128, 4 * LS], BF16)
    Abuf = [sb(f"A{i}", [128, IGC * LS], BF16) for i in range(2)]
    Wsl = [sb(f"W{i}", [128, 8192], BF16) for i in range(4)]
    Dsl = [sb(f"Dn{i}", [128, IGC * 256], BF16) for i in range(2)]
    kcb = [sb(f"kc{i}", [128, LS], F32) for i in range(2)]
    ugb = [sb(f"ug{i}", [128, LS], BF16) for i in range(2)]
    vgb = [sb(f"vg{i}", [128, LS], BF16) for i in range(2)]
    gtb = [sb(f"gt{i}", [128, LS], BF16) for i in range(2)]
    tmpb = [sb(f"tp{i}", [128, LSH], F32) for i in range(2)]
    xsqb = [sb(f"xq{i}", [128, LSH], BF16) for i in range(2)]
    abc = sb("abc", [128, LSH], F32)
    bbc = sb("bbc", [128, LSH], F32)
    cbc = [sb(f"cb{i}", [128, LS], BF16) for i in range(2)]
    sgbc = sb("sgbc", [128, LS], BF16)
    rows = sb("rows", [1, 4 * LSH], F32)
    smx = [sb(f"sx{i}", [128, 8], F32) for i in range(2)]
    msc = [sb(f"ms{i}", [128, 8], F32) for i in range(2)]
    red = [sb(f"rd{i}", [128, 8], F32) for i in range(2)]
    cmb = [sb(f"cm{i}", [128, 9], BF16) for i in range(2)]
    rlog = [sb(f"rl{i}", [128, 12], F32) for i in range(4)]
    identW = sb("identW", [128, 128], BF16)
    crst = sb("crst", [9, 512], BF16)
    bub = sb("bub", [128, 16], F32)
    bvb = sb("bvb", [128, 16], F32)
    bg1b = sb("bg1b", [128, 4], F32)
    bg2b = sb("bg2b", [128, 16], F32)
    cbb = sb("cbb", [128, 16], F32)
    cwb = sb("cwb", [128, 48], F32)
    wrb = sb("wrb", [128, 144], BF16)
    ones = sb("ones", [128, 2], BF16)
    epsb = sb("epsb", [1, 2], F32)
    onesf = sb("onesf", [1, 128], F32)
    stg = [sb(f"stg{i}", [128, 512], F32) for i in range(4)]
    carry = sb("carry", [128, 16], F32)

    psm = [nc.alloc_psum_tensor(f"pm{i}", [128, 512], F32) for i in range(4)]
    psa = [nc.alloc_psum_tensor(f"pa{i}", [128, 512], F32) for i in range(2)]
    pst = nc.alloc_psum_tensor("pst", [1, 1024], F32)

    b = Builder(nc)
    SEM = b.sem

    accB = [Buf(f"acc{c}") for c in range(NCH)]
    lnB = [Buf(f"ln{c}") for c in range(NCH)]
    xbB = [Buf(f"xb{c}") for c in range(NCH)]
    hB = Buf("h")
    AB = [Buf(f"A{i}") for i in range(2)]
    WB = [Buf(f"W{i}") for i in range(4)]
    DB = [Buf(f"D{i}") for i in range(2)]
    kcB = [Buf(f"kc{i}") for i in range(2)]
    ugB = [Buf(f"ug{i}") for i in range(2)]
    vgB = [Buf(f"vg{i}") for i in range(2)]
    gtB = [Buf(f"gt{i}") for i in range(2)]
    tmpB = [Buf(f"tp{i}") for i in range(2)]
    xsqB = [Buf(f"xq{i}") for i in range(2)]
    abcB, bbcB = Buf("abc"), Buf("bbc")
    cbcB = [Buf(f"cb{i}") for i in range(2)]
    sgbcB = Buf("sgbc")
    rowsB = Buf("rows")
    smxB = [Buf(f"sx{i}") for i in range(2)]
    mscB = [Buf(f"ms{i}") for i in range(2)]
    redB = [Buf(f"rd{i}") for i in range(2)]
    cmbB = [Buf(f"cm{i}") for i in range(2)]
    rlogB = [Buf(f"rl{i}") for i in range(4)]
    crstB = Buf("crst")
    constB = Buf("consts")
    stgB = [Buf(f"stg{i}") for i in range(4)]
    wrB = Buf("wrb")
    carryB = [Buf(f"cy{c}") for c in range(NCH)]
    psmB = [Buf(f"pm{i}") for i in range(4)]
    psaB = [Buf(f"pa{i}") for i in range(2)]
    pstB = Buf("pst")
    cacheB = {r["key"]: Buf(str(r["key"])) for r in regs}
    lnrowB = Buf("lnrow")
    crowB = Buf("crow")
    outB = Buf("out_t")

    psm_rot = [0]

    def next_psm():
        i = psm_rot[0] % 4
        psm_rot[0] += 1
        return i

    def acc_full(c):
        return acc[:, c * LSH + 1:c * LSH + 515]

    def acc_own(c):
        return acc[:, c * LSH + 2:c * LSH + 2 + LS]

    def acc_chunk(c):
        return acc[:, c * LSH:(c + 1) * LSH]

    def ln_full(c):
        return lnb[:, c * LSH + 1:c * LSH + 515]

    def ln_chunk(c):
        return lnb[:, c * LSH:(c + 1) * LSH]

    def ln_own(c):
        return lnb[:, c * LSH + 2:c * LSH + 2 + LS]

    def xb_ch(c):
        return xbuf[:, c * LS:(c + 1) * LS]

    def run_mms(e, mms):
        ins = None
        for (o, lt, rh, st, sp) in mms:
            ins = e.matmul(o, lt, rh, start=st, stop=sp)
        ins.then_inc(SEM["PE"], 1)

    def peg(psb, mms, reads):
        b.op("PE", (lambda mms=tuple(mms): (lambda e: run_mms(e, mms)))(),
             reads=list(reads), writes=[psb])

    def dve(emit1, reads, writes):
        def emit(e, f=emit1):
            f(e).then_inc(SEM["DVE"], 1)
        return b.op("DVE", emit, reads=list(reads), writes=list(writes))

    def act(emit1, reads, writes):
        def emit(e, f=emit1):
            f(e).then_inc(SEM["ACT"], 1)
        return b.op("ACT", emit, reads=list(reads), writes=list(writes))

    qrot = {q: 0 for q in QSLOTS}
    quseB = {q: [Buf(f"{q}use{i}") for i in range(n)] for q, n in QSLOTS.items()}

    def dma(q, pairs, reads, writes, **kw):
        k = qrot[q] % QSLOTS[q]
        qrot[q] += 1
        dom = f"{q}{k}"

        def emit(e, pairs=tuple(pairs), kw=kw, dom=dom):
            for (dst, src) in pairs:
                e.dma_start(dst, src, **kw).then_inc(SEM[dom], 16)
        return b.op(dom, emit, reads=list(reads),
                    writes=list(writes) + [quseB[q][k]],
                    inc=16 * len(pairs))

    cast_rot = [0]

    def cast_piece(ceng, dst_ap, src_ap, reads, writes):
        if ceng == "DVE":
            dve(lambda e: e.tensor_copy(dst_ap, src_ap), reads, writes)
        elif ceng == "ACT":
            act(lambda e: e.copy(dst_ap, src_ap), reads, writes)
        else:
            def emit(e):
                e.tensor_copy(dst_ap, src_ap).then_inc(SEM["PL"], 1)
            b.op("PL", emit, reads=list(reads), writes=list(writes))

    wbrot = [0]

    def wqueue(key):
        return "SP"

    def load_region(seg, key, slot_i, dslot=False):
        r = rmap[key]
        w = r["kc"] * r["cols"]
        off = r["off"]
        slot = (Dsl if dslot else Wsl)[slot_i]
        dstap = slot[:, 0:w]
        slotB = (DB if dslot else WB)[slot_i]
        cB = cacheB[key]
        q = wqueue(key)
        if seg == 0:
            ceng = ("DVE", "ACT", "PL")[cast_rot[0] % 3]
            cast_rot[0] += 1
            poff = 0
            qbase = 0 if q == "SP" else 2
            while poff < w:
                pw = min(512, w - poff)
                pp = qbase + (poff // 512) % 2
                dma(q, [(stg[pp][:, 0:pw],
                         wflat[:, off + poff:off + poff + pw])],
                    reads=[], writes=[stgB[pp]])
                cast_piece(ceng, slot[:, poff:poff + pw], stg[pp][:, 0:pw],
                           [stgB[pp]], [slotB])
                poff += pw
            wbq = "ACTQ" if (wbrot[0] % 2) else "SP"
            wbrot[0] += 1
            dma(wbq, [(cflat[:, off:off + w], dstap)],
                reads=[slotB], writes=[cB])
        else:
            dma(q, [(dstap, cflat[:, off:off + w])], reads=[cB], writes=[slotB])
        return r

    def lhsT(slot_i, r, k, j, dslot=False):
        slot = (Dsl if dslot else Wsl)[slot_i]
        base = k * r["cols"] + j * 128
        return slot[:, base:base + 128]

    def emit_pro(e):
        e.memset(carry[:, :], 0.0)
        e.memset(epsb[:, :], EPS)
        e.memset(onesf[:, :], 1.0)
        e.memset(ones[:, :], 1.0).then_inc(SEM["DVE"], 4)
    tok = b.op("DVE", emit_pro, reads=[], writes=[constB], inc=4)
    for cc in carryB:
        cc.w = tok
    dma("ACTQ", [(bub[:, :], bu16), (bvb[:, :], bv16), (bg1b[:, :], bg1d),
                 (bg2b[:, :], bg2d), (cbb[:, :], cbd), (cwb[:, :], cwd)],
        reads=[], writes=[constB])
    dma("SP", [(stg[0][:, 0:144], wr9)], reads=[], writes=[stgB[0]])
    dve(lambda e: e.tensor_copy(wrb[:, :], stg[0][:, 0:144]),
        reads=[stgB[0]], writes=[wrB])
    dma("SP", [(stg[1][:, 0:128], identd)], reads=[], writes=[stgB[1]])
    dve(lambda e: e.tensor_copy(identW[:, :], stg[1][:, 0:128]),
        reads=[stgB[1]], writes=[wrB])

    def peg_stats(mms, reads):
        b.op("PE", (lambda mms=tuple(mms): (lambda e: run_mms(e, mms)))(),
             reads=list(reads), writes=[pstB])

    def ln_cast(which, c):
        src = acc_full if which == 1 else acc_own
        dstf = ln_full if which == 1 else ln_own
        dve(lambda e, c=c: e.tensor_copy(dstf(c), src(c)),
            reads=[accB[c]], writes=[lnB[c]])

    def layernorm(which, skip_casts=False):
        n = 514 if which == 1 else LS
        src = acc_full if which == 1 else acc_own
        dstf = ln_full if which == 1 else ln_own
        if not skip_casts:
            for c in range(NCH):
                ln_cast(which, c)
        for c in range(NCH):
            mms = [(pst[0:1, 0:512], ones[:, 0:1], dstf(c)[:, 0:512],
                    c == 0, c == NCH - 1)]
            if which == 1:
                mms.append((pst[0:1, 512:514], ones[:, 0:1],
                            dstf(c)[:, 512:514], c == 0, c == NCH - 1))
            peg_stats(mms, [lnB[c], constB])
        act(lambda e: e.copy(rows[:, 0:n], pst[0:1, 0:n]),
            reads=[pstB], writes=[rowsB])
        for c in range(NCH):
            p = c % 2
            act(lambda e, c=c, p=p: e.activation(
                xsqb[p][:, 0:n], src(c), AF.Square, bias=0.0, scale=1.0),
                reads=[accB[c]], writes=[xsqB[p]])
            mms = [(pst[0:1, 0:512], ones[:, 0:1], xsqb[p][:, 0:512],
                    c == 0, c == NCH - 1)]
            if which == 1:
                mms.append((pst[0:1, 512:514], ones[:, 0:1],
                            xsqb[p][:, 512:514], c == 0, c == NCH - 1))
            peg_stats(mms, [xsqB[p], constB])
        act(lambda e: e.copy(rows[:, LSH:LSH + n], pst[0:1, 0:n]),
            reads=[pstB], writes=[rowsB])
        S = rows[:, 0:n]
        Q = rows[:, LSH:LSH + n]
        A_ = rows[:, 2 * LSH:2 * LSH + n]
        B_ = rows[:, 3 * LSH:3 * LSH + n]
        dve(lambda e: e.tensor_mul(A_, S, S), reads=[rowsB], writes=[rowsB])
        dve(lambda e: e.scalar_tensor_tensor(Q, A_, -1.0 / D, Q, AL.mult, AL.add),
            reads=[rowsB], writes=[rowsB])
        act(lambda e: e.activation(A_, Q, AF.Ln, bias=epsb[:, 0:1], scale=1.0 / D),
            reads=[rowsB, constB], writes=[rowsB])
        act(lambda e: e.activation(Q, A_, AF.Exp, bias=0.0, scale=-0.5),
            reads=[rowsB], writes=[rowsB])
        dve(lambda e: e.scalar_tensor_tensor(B_, S, -1.0 / D, Q, AL.mult, AL.mult),
            reads=[rowsB], writes=[rowsB])
        for row_ap, dstt, dstB in ((Q, abc, abcB), (B_, bbc, bbcB)):
            mm1 = [(psa[0][:, 0:512], onesf[0:1, :], row_ap[0:1, 0:512],
                    True, True)]
            b.op("PE", (lambda mms=tuple(mm1): (lambda e: run_mms(e, mms)))(),
                 reads=[rowsB, constB], writes=[psaB[0]])
            act(lambda e, dstt=dstt: e.copy(dstt[:, 0:512], psa[0][:, 0:512]),
                reads=[psaB[0]], writes=[dstB])
            if n > 512:
                mm2 = [(psa[1][:, 0:n - 512], onesf[0:1, :],
                        row_ap[0:1, 512:n], True, True)]
                b.op("PE", (lambda mms=tuple(mm2): (lambda e: run_mms(e, mms)))(),
                     reads=[rowsB, constB], writes=[psaB[1]])
                act(lambda e, dstt=dstt, n=n: e.copy(dstt[:, 512:n],
                                                     psa[1][:, 0:n - 512]),
                    reads=[psaB[1]], writes=[dstB])
        for c in range(NCH):
            p = c % 2
            dve(lambda e, c=c, p=p: e.tensor_mul(
                tmpb[p][:, 0:n], src(c), abc[:, 0:n]),
                reads=[accB[c], abcB], writes=[tmpB[p]])
            dve(lambda e, c=c, p=p: e.tensor_add(
                dstf(c), tmpb[p][:, 0:n], bbc[:, 0:n]),
                reads=[tmpB[p], bbcB], writes=[lnB[c]])

    def dn_load(seg, g, si):
        return load_region(seg, ("dn", g, si), si % 2, dslot=True)

    def emit_down(seg, g, smap):
        gpar = g % 2
        for dc in range(NCH):
            si = dc // 2
            dsl_i, rdn = smap[("dn", si)]
            pa_i = dc % 2
            mms = [(psa[pa_i][:, :], lhsT(dsl_i, rdn, k, dc % 2, dslot=True),
                    Abuf[gpar][:, k * LS:(k + 1) * LS],
                    k == 0, k == IGC - 1) for k in range(IGC)]
            b.op("PE", (lambda mms=tuple(mms): (lambda e: run_mms(e, mms)))(),
                 reads=[DB[dsl_i], AB[gpar]], writes=[psaB[pa_i]])
            dve(lambda e, dc=dc, pa_i=pa_i: e.tensor_add(
                acc_own(dc), acc_own(dc), psa[pa_i][:, :]),
                reads=[psaB[pa_i]], writes=[accB[dc]])
            if dc % 2 == 1 and si + 2 <= 7:
                smap[("dn", si + 2)] = ((si + 2) % 2, dn_load(seg, g, si + 2))


    def emit_softmax(p, t, nm, sm, rs, m1, m2, cpairs):
        rl = rlog[t]
        dve(lambda e: e.reduce_max(nm, rl[:, 0:8], axis=AX, negate=True),
            reads=[rlogB[t]], writes=[redB[p]])
        act(lambda e: e.activation(smx[p][:, :], rl[:, 0:8], AF.Exp, bias=nm,
                                   scale=1.0, accum_out=sm),
            reads=[rlogB[t], redB[p]], writes=[smxB[p], redB[p]])
        act(lambda e: e.activation(cmb[p][:, 8:9], rl[:, 8:9], AF.Sigmoid,
                                   bias=0.0, scale=1.0),
            reads=[rlogB[t]], writes=[cmbB[p]])
        dve(lambda e: e.reciprocal(rs, sm), reads=[redB[p]], writes=[redB[p]])
        dve(lambda e: e.tensor_scalar(smx[p][:, :], smx[p][:, :], rs, None,
                                      AL.mult),
            reads=[smxB[p], redB[p]], writes=[smxB[p]])
        dve(lambda e: e.reduce_max(m1, smx[p][:, :], axis=AX),
            reads=[smxB[p]], writes=[redB[p]])
        dve(lambda e: e.tensor_scalar(msc[p][:, :], smx[p][:, :], m1, None,
                                      AL.is_lt),
            reads=[smxB[p], redB[p]], writes=[mscB[p]])
        dve(lambda e: e.tensor_mul(msc[p][:, :], smx[p][:, :], msc[p][:, :]),
            reads=[smxB[p], mscB[p]], writes=[mscB[p]])
        dve(lambda e: e.reduce_max(m2, msc[p][:, :], axis=AX),
            reads=[mscB[p]], writes=[redB[p]])
        dve(lambda e: e.tensor_scalar(msc[p][:, :], smx[p][:, :], m2, None,
                                      AL.is_ge),
            reads=[smxB[p], redB[p]], writes=[mscB[p]])
        dve(lambda e: e.tensor_mul(cmb[p][:, 0:8], smx[p][:, :], msc[p][:, :]),
            reads=[smxB[p], mscB[p]], writes=[cmbB[p]])
        ti = next_psm()
        tview = psm[ti][:, :].bitcast(BF16)
        def emit_tr(e, p=p, ti=ti, tview=tview):
            e.transpose(tview[0:9, 0:128], cmb[p][:, 0:9],
                        identW[:, :]).then_inc(SEM["PE"], 1)
        b.op("PE", emit_tr, reads=[cmbB[p], constB], writes=[psmB[ti]])
        t4 = cpairs
        tt = t4[0]
        act(lambda e, ti=ti, tview=tview, tt=tt: e.copy(
            crst[:, tt * 128:(tt + 1) * 128], tview[0:9, 0:128]),
            reads=[psmB[ti]], writes=[crstB])

    for seg in range(NSEG):
        l0 = seg * LS
        pairs = []
        for c in range(NCH):
            sa = max(l0 - 1, 0)
            sbnd = min(l0 + LS + 1, L)
            doff = 2 - (l0 - sa)
            pairs.append((acc[:, c * LSH + doff:c * LSH + doff + (sbnd - sa)],
                          xt[c * 128:(c + 1) * 128, sa:sbnd]))
        for gq in range(4):
            dma("ACTQ", pairs[gq * 4:(gq + 1) * 4], reads=[],
                writes=accB[gq * 4:(gq + 1) * 4])
        if seg == 0:
            dve(lambda e: e.memset(acc[:, 1:NCH * LSH:LSH], 0.0),
                reads=[], writes=accB)
        if seg == NSEG - 1:
            dve(lambda e: e.memset(acc[:, 514:NCH * LSH:LSH], 0.0),
                reads=[], writes=accB)

        layernorm(1)

        acyc = [0]

        def aslot():
            i = acyc[0] % 3
            acyc[0] += 1
            return i
        rg2 = load_region(seg, ("wg2",), 3)
        rg1 = load_region(seg, ("wg1",), aslot())

        for c in range(NCH):
            p = c % 2
            w0 = cwb[:, 3 * c:3 * c + 1]
            w1 = cwb[:, 3 * c + 1:3 * c + 2]
            w2 = cwb[:, 3 * c + 2:3 * c + 3]
            cb_ = cbb[:, c:c + 1]
            lf = ln_chunk(c)
            dve(lambda e, p=p, w0=w0, cb_=cb_, lf=lf: e.tensor_scalar(
                tmpb[p][:, 0:LS], lf[:, 1:513], w0, cb_, AL.mult, AL.add),
                reads=[lnB[c], constB], writes=[tmpB[p]])
            dve(lambda e, p=p, w1=w1, lf=lf: e.scalar_tensor_tensor(
                tmpb[p][:, 0:LS], lf[:, 2:514], w1, tmpb[p][:, 0:LS],
                AL.mult, AL.add),
                reads=[lnB[c], constB, tmpB[p]], writes=[tmpB[p]])
            dve(lambda e, p=p, w2=w2, lf=lf: e.scalar_tensor_tensor(
                tmpb[p][:, 0:LS], lf[:, 3:515], w2, tmpb[p][:, 0:LS],
                AL.mult, AL.add),
                reads=[lnB[c], constB, tmpB[p]], writes=[tmpB[p]])
            dve(lambda e, c=c, p=p: e.tensor_add(
                acc_own(c), acc_own(c), tmpb[p][:, 0:LS]),
                reads=[tmpB[p]], writes=[accB[c]])

        for hc in range(4):
            pi = next_psm()
            mms = [(psm[pi][:, :], lhsT(0, rg1, k, hc), ln_own(k),
                    k == 0, k == NCH - 1) for k in range(NCH)]
            peg(psmB[pi], mms, [WB[0]] + lnB)
            act(lambda e, hc=hc, pi=pi: e.activation(
                hbuf[:, hc * LS:(hc + 1) * LS], psm[pi][:, :], AF.Relu,
                bias=bg1b[:, hc:hc + 1], scale=1.0),
                reads=[psmB[pi], constB], writes=[hB])

        cur = {}
        for rc in range(NCH):
            blk = rc // 4
            if rc % 4 == 0:
                su = aslot()
                ru = load_region(seg, ("wu", blk), su)
                sv = aslot()
                rv = load_region(seg, ("wv", blk), sv)
                cur[blk] = (su, ru, sv, rv)
            su, ru, sv, rv = cur[blk]
            p = rc % 2
            pg = next_psm()
            mms = [(psm[pg][:, :], lhsT(3, rg2, hk, rc),
                    hbuf[:, hk * LS:(hk + 1) * LS], hk == 0, hk == 3)
                   for hk in range(4)]
            peg(psmB[pg], mms, [WB[3], hB])
            act(lambda e, rc=rc, pg=pg, p=p: e.activation(
                gtb[p][:, :], psm[pg][:, :], AF.Sigmoid,
                bias=bg2b[:, rc:rc + 1], scale=1.0),
                reads=[psmB[pg], constB], writes=[gtB[p]])
            pu = next_psm()
            mms = [(psm[pu][:, :], lhsT(su, ru, k, rc % 4), ln_own(k),
                    k == 0, k == NCH - 1) for k in range(NCH)]
            peg(psmB[pu], mms, [WB[su]] + lnB)
            dve(lambda e, rc=rc, pu=pu, p=p: e.scalar_tensor_tensor(
                ugb[p][:, :], psm[pu][:, :], bub[:, rc:rc + 1], gtb[p][:, :],
                AL.add, AL.mult),
                reads=[psmB[pu], constB, gtB[p]], writes=[ugB[p]])
            pv = next_psm()
            mms = [(psm[pv][:, :], lhsT(sv, rv, k, rc % 4), ln_own(k),
                    k == 0, k == NCH - 1) for k in range(NCH)]
            peg(psmB[pv], mms, [WB[sv]] + lnB)
            dve(lambda e, rc=rc, pv=pv, p=p: e.scalar_tensor_tensor(
                vgb[p][:, :], psm[pv][:, :], bvb[:, rc:rc + 1], gtb[p][:, :],
                AL.add, AL.mult),
                reads=[psmB[pv], constB, gtB[p]], writes=[vgB[p]])
            dve(lambda e, rc=rc, p=p: e.tensor_tensor_scan(
                kcb[p][:, :], ugb[p][:, :], ugb[p][:, :],
                carry[:, rc:rc + 1], AL.add, AL.bypass),
                reads=[ugB[p], carryB[rc]], writes=[kcB[p]])
            dve(lambda e, rc=rc, p=p: e.tensor_copy(
                carry[:, rc:rc + 1], kcb[p][:, LS - 1:LS]),
                reads=[kcB[p]], writes=[carryB[rc]])
            dve(lambda e, rc=rc, p=p: e.tensor_mul(
                xb_ch(rc), kcb[p][:, :], vgb[p][:, :]),
                reads=[kcB[p], vgB[p]], writes=[xbB[rc]])

        for dc in range(NCH):
            blk = dc // 4
            if dc % 4 == 0:
                si_uv = aslot()
                ruv = load_region(seg, ("wu", blk), si_uv)
                cur[("uv", blk)] = (si_uv, ruv)
            si_uv, ruv = cur[("uv", blk)]
            pa_i = dc % 2
            mms = [(psa[pa_i][:, :], lhsT(si_uv, ruv, k, dc % 4), xb_ch(k),
                    k == 0, k == NCH - 1) for k in range(NCH)]
            b.op("PE", (lambda mms=tuple(mms): (lambda e: run_mms(e, mms)))(),
                 reads=[WB[si_uv]] + xbB, writes=[psaB[pa_i]])
            dve(lambda e, dc=dc, pa_i=pa_i: e.tensor_add(
                acc_own(dc), acc_own(dc), psa[pa_i][:, :]),
                reads=[psaB[pa_i]], writes=[accB[dc]])
            ln_cast(2, dc)

        layernorm(2, skip_casts=True)

        router_tail = []
        for t in range(4):
            p = t % 2
            pr_i = next_psm()
            mms = [(psm[pr_i][:, 0:9],
                    lnb[:, k * LSH + 2 + t * 128:k * LSH + 2 + (t + 1) * 128],
                    wrb[:, k * 9:(k + 1) * 9], k == 0, k == NCH - 1)
                   for k in range(NCH)]
            peg(psmB[pr_i], mms, lnB + [wrB])
            act(lambda e, t=t, pr_i=pr_i: e.copy(rlog[t][:, 0:9],
                                                 psm[pr_i][:, 0:9]),
                reads=[psmB[pr_i]], writes=[rlogB[t]])
            nm = red[p][:, 0:1]
            sm = red[p][:, 1:2]
            rs = red[p][:, 2:3]
            m1 = red[p][:, 3:4]
            m2 = red[p][:, 4:5]
            router_tail.append(
                (lambda p=p, t=t, nm=nm, sm=sm, rs=rs, m1=m1, m2=m2:
                 emit_softmax(p, t, nm, sm, rs, m1, m2, (t,))))

        def emit_x2s():
            dma("ROWQ", [(crow[0:9, l0:l0 + LS], crst[:, :])],
                reads=[crstB], writes=[crowB])
            dma("ROWQ", [(sgbc[:, :],
                          crow[8:9, l0:l0 + LS].broadcast_to([128, LS]))],
                reads=[crowB], writes=[sgbcB])
            for c in range(NCH):
                def emit_xs(e, c=c):
                    e.tensor_mul(xb_ch(c), ln_own(c),
                                 sgbc[:, :]).then_inc(SEM["PL"], 1)
                b.op("PL", emit_xs, reads=[lnB[c], sgbcB], writes=[xbB[c]])

        fcyc = [0]

        def fslot():
            i = [3, 0, 1, 2][fcyc[0] % 4]
            fcyc[0] += 1
            return i
        gsmaps = {}
        for g in range(NGRP):
            smap = {}
            gsmaps[g] = smap
            if g >= 1:
                pm1 = gsmaps[g - 1]
                pm1[("dn", 0)] = (0, dn_load(seg, g - 1, 0))
                pm1[("dn", 1)] = (1, dn_load(seg, g - 1, 1))
            gpar = g % 2
            for ii in range(IGC):
                si = ii // 4
                if ii % 4 == 0:
                    sl = fslot()
                    smap[("gate", si)] = (sl, load_region(seg, ("gate", g, si), sl))
                sl, r_ = smap[("gate", si)]
                pg = next_psm()
                mms = [(psm[pg][:, :], lhsT(sl, r_, k, ii - 4 * si), ln_own(k),
                        k == 0, k == NCH - 1) for k in range(NCH)]
                peg(psmB[pg], mms, [WB[sl]] + lnB)
                act(lambda e, ii=ii, pg=pg, gpar=gpar: e.activation(
                    Abuf[gpar][:, ii * LS:(ii + 1) * LS], psm[pg][:, :],
                    AF.Silu, bias=0.0, scale=1.0),
                    reads=[psmB[pg]], writes=[AB[gpar]])
            if g == 0:
                for f in router_tail:
                    f()
                emit_x2s()
            for ii in range(IGC):
                si = ii // 4
                if ii % 4 == 0:
                    sl = fslot()
                    smap[("up", si)] = (sl, load_region(seg, ("up", g, si), sl))
                sl, r_ = smap[("up", si)]
                pc = next_psm()
                mms = [(psm[pc][:, :], lhsT(sl, r_, k, ii - 4 * si), xb_ch(k),
                        k == 0, k == NCH - 1) for k in range(NCH)]
                b.op("PE", (lambda mms=tuple(mms): (lambda e: run_mms(e, mms)))(),
                     reads=[WB[sl]] + xbB, writes=[psmB[pc]])
                dve(lambda e, ii=ii, pc=pc, gpar=gpar: e.tensor_mul(
                    Abuf[gpar][:, ii * LS:(ii + 1) * LS],
                    psm[pc][:, :], Abuf[gpar][:, ii * LS:(ii + 1) * LS]),
                    reads=[psmB[pc], AB[gpar]], writes=[AB[gpar]])
            if 4 <= g + 1 < NGRP:
                e_next = g + 1 - 4
                cp = e_next % 2
                dma("ROWQ", [(cbc[cp][:, :],
                              crow[e_next:e_next + 1, l0:l0 + LS]
                              .broadcast_to([128, LS]))],
                    reads=[crowB], writes=[cbcB[cp]])
                for c in range(NCH):
                    def emit_xc(e, c=c, cp=cp):
                        e.tensor_mul(xb_ch(c), ln_own(c), cbc[cp][:, :]).then_inc(SEM["PL"], 1)
                    b.op("PL", emit_xc, reads=[lnB[c], cbcB[cp]], writes=[xbB[c]])
            if g >= 1:
                emit_down(seg, g - 1, gsmaps[g - 1])
        gsmaps[NGRP - 1][("dn", 0)] = (0, dn_load(seg, NGRP - 1, 0))
        gsmaps[NGRP - 1][("dn", 1)] = (1, dn_load(seg, NGRP - 1, 1))
        emit_down(seg, NGRP - 1, gsmaps[NGRP - 1])

        spairs = [(out_t[c * 128:(c + 1) * 128, l0:l0 + LS], acc_own(c))
                  for c in range(NCH)]
        for gq in range(4):
            dma("ACTQ", spairs[gq * 4:(gq + 1) * 4],
                reads=accB[gq * 4:(gq + 1) * 4], writes=[outB])

    b.finish()
    with nc.Block() as block:
        b.realize(block)
    return nc


def kernel(**inputs):
    regs, TOTW = make_regions()
    wflat = pack_host(inputs, regs, TOTW)
    bu16 = np.ascontiguousarray(inputs["L0_attn_bu"].reshape(16, 128).T)
    bv16 = np.ascontiguousarray(inputs["L0_attn_bv"].reshape(16, 128).T)
    bg1d = np.ascontiguousarray(inputs["L0_attn_bg1"].reshape(4, 128).T)
    bg2d = np.ascontiguousarray(inputs["L0_attn_bg2"].reshape(16, 128).T)
    cbd = np.ascontiguousarray(inputs["L0_attn_conv_b"].reshape(16, 128).T)
    cwd = np.ascontiguousarray(
        inputs["L0_attn_conv_w"].reshape(16, 128, 3)
        .transpose(1, 0, 2).reshape(128, 48))
    wr9 = np.ascontiguousarray(
        np.concatenate([inputs["L0_ffn_W_router"], inputs["L0_ffn_W_sgate"]], 0)
        .T.reshape(16, 128, 9).transpose(1, 0, 2).reshape(128, 144))
    ident = np.eye(128, dtype=np.float32)
    x = inputs["L0_attn_x"]

    nc = build_program()
    in_maps = []
    for c in range(8):
        in_maps.append(dict(
            xt=np.ascontiguousarray(x[c].T), wflat=wflat, bu16=bu16,
            bv16=bv16, bg1d=bg1d, bg2d=bg2d, cbd=cbd, cwd=cwd, wr9=wr9,
            identd=ident))
    res = run_bass_kernel_spmd(nc, in_maps, core_ids=list(range(8)))
    out = np.stack([np.ascontiguousarray(r["out_t"].T) for r in res.results])
    return out.astype(np.float32)


# version 06
# speedup vs baseline: 1.4573x; 1.0614x over previous
import bisect
import numpy as np
import concourse.bass as bass
import concourse.mybir as mybir
from concourse.bass_utils import run_bass_kernel_spmd

F32 = mybir.dt.float32
BF16 = mybir.dt.bfloat16
AL = mybir.AluOpType
AF = mybir.ActivationFunctionType
AX = mybir.AxisListType.X

D = 2048
L = 4096
NCH = 16
LS = 512
NSEG = L // LS
LSH = LS + 4
IG = 1408
IGC = 11
NGRP = 12
EPS = 1e-5

QSLOTS = {"SP": 12, "ACTQ": 12, "ROWQ": 4}
QENG = {"SP": "SP", "ACTQ": "ACT", "ROWQ": "ACT"}
DOMS = ["PE", "ACT", "DVE", "PL"] + [f"{q}{i}" for q, n in QSLOTS.items()
                                     for i in range(n)]
DOM2ENG = {"PE": "PE", "ACT": "ACT", "DVE": "DVE", "PL": "PL"}
for q, n in QSLOTS.items():
    for i in range(n):
        DOM2ENG[f"{q}{i}"] = QENG[q]
ENG2BASS = {"PE": "tensor", "ACT": "scalar", "DVE": "vector", "PL": "gpsimd",
            "SP": "sync"}


class Buf:
    __slots__ = ("name", "w", "r")

    def __init__(self, name):
        self.name = name
        self.w = None
        self.r = {}


class Builder:
    def __init__(self, nc):
        self.nc = nc
        self.sem = {d: nc.alloc_semaphore(f"sm_{d}") for d in DOMS}
        self.cnt = {d: 0 for d in DOMS}
        self.prog = {e: [] for e in ENG2BASS}
        self.know = {e: {d: 0 for d in DOMS} for e in ENG2BASS}
        self.snapc = {d: [] for d in DOMS}
        self.snapk = {d: [] for d in DOMS}

    def _wait(self, eng, dom, val):
        if val <= 0 or val <= self.know[eng].get(dom, 0):
            return
        self.prog[eng].append(("w", dom, val))
        k = self.know[eng]
        k[dom] = max(k[dom], val)
        i = bisect.bisect_left(self.snapc[dom], val)
        if i < len(self.snapc[dom]):
            for d2, v2 in self.snapk[dom][i].items():
                if v2 > k[d2]:
                    k[d2] = v2

    def op(self, dom, emit, reads=(), writes=(), inc=1):
        eng = DOM2ENG[dom]
        deps = {}
        for b in reads:
            t = b.w
            if t is not None and t[1] > deps.get(t[0], 0):
                deps[t[0]] = t[1]
        for b in writes:
            t = b.w
            if t is not None and t[1] > deps.get(t[0], 0):
                deps[t[0]] = t[1]
            for d2, v in b.r.items():
                if v > deps.get(d2, 0):
                    deps[d2] = v
        for d2 in sorted(deps):
            if eng == "PE" and DOM2ENG[d2] == "PE":
                continue
            self._wait(eng, d2, deps[d2])
        self.prog[eng].append(("op", emit))
        self.cnt[dom] += inc
        c = self.cnt[dom]
        tok = (dom, c)
        for b in reads:
            if b.r.get(dom, 0) < c:
                b.r[dom] = c
        for b in writes:
            b.w = tok
            b.r = {}
        self.snapc[dom].append(c)
        self.snapk[dom].append(dict(self.know[eng]))
        return tok

    def finish(self):
        for eng in ENG2BASS:
            for d in DOMS:
                if self.cnt[d] > 0:
                    self._wait(eng, d, self.cnt[d])

    def realize(self, block):
        sem = self.sem
        for eng, seq in self.prog.items():
            sec = getattr(block, ENG2BASS[eng])

            def mk(seq=seq):
                def body(e):
                    for it in seq:
                        if it[0] == "w":
                            e.wait_ge(sem[it[1]], it[2])
                        else:
                            it[1](e)
                return body
            sec(mk())


def make_regions():
    regs = []
    off = [0]

    def add(key, kc, cols):
        r = dict(key=key, kc=kc, cols=cols, off=off[0])
        off[0] += kc * cols
        regs.append(r)
        return r
    add(("wg2",), 4, 2048)
    add(("wg1",), 16, 512)
    for bk in range(4):
        add(("wu", bk), 16, 512)
    for bk in range(4):
        add(("wv", bk), 16, 512)
    for g in range(NGRP):
        for si, w in ((0, 512), (1, 512), (2, 384)):
            add(("gate", g, si), 16, w)
        for si, w in ((0, 512), (1, 512), (2, 384)):
            add(("up", g, si), 16, w)
        for si in range(8):
            add(("dn", g, si), IGC, 256)
    return regs, off[0]


def pack_host(inp, regs, total):
    wu_t = np.ascontiguousarray(inp["L0_attn_Wu"].T)
    wv_t = np.ascontiguousarray(inp["L0_attn_Wv"].T)
    wg1_t = np.ascontiguousarray(inp["L0_attn_Wg1"].T)
    wg2_t = np.ascontiguousarray(inp["L0_attn_Wg2"].T)
    sgu = inp["L0_ffn_Ws_gate_up"]
    sdn = inp["L0_ffn_Ws_down"]
    egu = inp["L0_ffn_We_gate_up"]
    edn = inp["L0_ffn_We_down"]

    def pm(mat, kc, c0, c1):
        return np.ascontiguousarray(
            mat.reshape(kc, 128, mat.shape[1])[:, :, c0:c1].transpose(1, 0, 2)
        ).reshape(128, kc * (c1 - c0))

    wflat = np.empty((128, total), np.float32)
    for r in regs:
        k = r["key"]
        kc, cols, off = r["kc"], r["cols"], r["off"]
        if k[0] == "wg2":
            piece = pm(wg2_t, 4, 0, 2048)
        elif k[0] == "wg1":
            piece = pm(wg1_t, 16, 0, 512)
        elif k[0] == "wu":
            piece = pm(wu_t, 16, k[1] * 512, (k[1] + 1) * 512)
        elif k[0] == "wv":
            piece = pm(wv_t, 16, k[1] * 512, (k[1] + 1) * 512)
        elif k[0] in ("gate", "up"):
            g, si = k[1], k[2]
            c0 = si * 512
            c1 = c0 + cols
            if g < 4:
                base = g * IG + (0 if k[0] == "gate" else 5632)
                piece = pm(sgu, 16, base + c0, base + c1)
            else:
                base = 0 if k[0] == "gate" else IG
                piece = pm(egu[g - 4], 16, base + c0, base + c1)
        else:
            g, si = k[1], k[2]
            c0, c1 = si * 256, si * 256 + 256
            if g < 4:
                piece = pm(sdn[g * IG:(g + 1) * IG], IGC, c0, c1)
            else:
                piece = pm(edn[g - 4], IGC, c0, c1)
        wflat[:, off:off + kc * cols] = piece
    return wflat


def build_program():
    nc = bass.Bass("TRN2", target_bir_lowering=False,
                   detect_race_conditions=False)
    regs, TOTW = make_regions()
    rmap = {r["key"]: r for r in regs}

    xt = nc.dram_tensor("xt", [D, L], F32, kind="ExternalInput").ap()
    wflat = nc.dram_tensor("wflat", [128, TOTW], F32, kind="ExternalInput").ap()
    bu16 = nc.dram_tensor("bu16", [128, 16], F32, kind="ExternalInput").ap()
    bv16 = nc.dram_tensor("bv16", [128, 16], F32, kind="ExternalInput").ap()
    bg1d = nc.dram_tensor("bg1d", [128, 4], F32, kind="ExternalInput").ap()
    bg2d = nc.dram_tensor("bg2d", [128, 16], F32, kind="ExternalInput").ap()
    cbd = nc.dram_tensor("cbd", [128, 16], F32, kind="ExternalInput").ap()
    cwd = nc.dram_tensor("cwd", [128, 48], F32, kind="ExternalInput").ap()
    wr9 = nc.dram_tensor("wr9", [128, 144], F32, kind="ExternalInput").ap()
    identd = nc.dram_tensor("identd", [128, 128], F32, kind="ExternalInput").ap()
    out_t = nc.dram_tensor("out_t", [D, L], F32, kind="ExternalOutput").ap()
    cflat = nc.dram_tensor("cflat", [128, TOTW], BF16, kind="Internal").ap()
    lnrow = nc.dram_tensor("lnrow", [2, LSH], F32, kind="Internal").ap()
    crow = nc.dram_tensor("crow", [9, L], BF16, kind="Internal").ap()

    sb = nc.alloc_sbuf_tensor
    acc = sb("acc", [128, NCH * LSH], F32)
    lnb = sb("lnb", [128, NCH * LSH], BF16)
    xbuf = sb("xbuf", [128, NCH * LS], BF16)
    hbuf = sb("hbuf", [128, 4 * LS], BF16)
    Abuf = [sb(f"A{i}", [128, IGC * LS], BF16) for i in range(2)]
    Wsl = [sb(f"W{i}", [128, 8192], BF16) for i in range(4)]
    Dsl = [sb(f"Dn{i}", [128, IGC * 256], BF16) for i in range(2)]
    kcb = [sb(f"kc{i}", [128, LS], F32) for i in range(2)]
    ugb = [sb(f"ug{i}", [128, LS], BF16) for i in range(2)]
    vgb = [sb(f"vg{i}", [128, LS], BF16) for i in range(2)]
    gtb = [sb(f"gt{i}", [128, LS], BF16) for i in range(2)]
    tmpb = [sb(f"tp{i}", [128, LSH], F32) for i in range(2)]
    xsqb = [sb(f"xq{i}", [128, LSH], BF16) for i in range(2)]
    abc = sb("abc", [128, LSH], F32)
    bbc = sb("bbc", [128, LSH], F32)
    cbc = [sb(f"cb{i}", [128, LS], BF16) for i in range(2)]
    sgbc = sb("sgbc", [128, LS], BF16)
    rows = sb("rows", [1, 4 * LSH], F32)
    smx = [sb(f"sx{i}", [128, 8], F32) for i in range(2)]
    msc = [sb(f"ms{i}", [128, 8], F32) for i in range(2)]
    red = [sb(f"rd{i}", [128, 8], F32) for i in range(2)]
    cmb = [sb(f"cm{i}", [128, 9], BF16) for i in range(2)]
    rlog = [sb(f"rl{i}", [128, 12], F32) for i in range(4)]
    identW = sb("identW", [128, 128], BF16)
    crst = sb("crst", [9, 512], BF16)
    bub = sb("bub", [128, 16], F32)
    bvb = sb("bvb", [128, 16], F32)
    bg1b = sb("bg1b", [128, 4], F32)
    bg2b = sb("bg2b", [128, 16], F32)
    cbb = sb("cbb", [128, 16], F32)
    cwb = sb("cwb", [128, 48], F32)
    wrb = sb("wrb", [128, 144], BF16)
    ones = sb("ones", [128, 2], BF16)
    epsb = sb("epsb", [1, 2], F32)
    onesf = sb("onesf", [1, 128], F32)
    stg = [sb(f"stg{i}", [128, 512], F32) for i in range(4)]
    carry = sb("carry", [128, 16], F32)

    psm = [nc.alloc_psum_tensor(f"pm{i}", [128, 512], F32) for i in range(4)]
    psa = [nc.alloc_psum_tensor(f"pa{i}", [128, 512], F32) for i in range(2)]
    pst = nc.alloc_psum_tensor("pst", [1, 1024], F32)

    b = Builder(nc)
    SEM = b.sem

    accB = [Buf(f"acc{c}") for c in range(NCH)]
    lnB = [Buf(f"ln{c}") for c in range(NCH)]
    xbB = [Buf(f"xb{c}") for c in range(NCH)]
    hB = Buf("h")
    AB = [Buf(f"A{i}") for i in range(2)]
    WB = [Buf(f"W{i}") for i in range(4)]
    DB = [Buf(f"D{i}") for i in range(2)]
    kcB = [Buf(f"kc{i}") for i in range(2)]
    ugB = [Buf(f"ug{i}") for i in range(2)]
    vgB = [Buf(f"vg{i}") for i in range(2)]
    gtB = [Buf(f"gt{i}") for i in range(2)]
    tmpB = [Buf(f"tp{i}") for i in range(2)]
    xsqB = [Buf(f"xq{i}") for i in range(2)]
    abcB, bbcB = Buf("abc"), Buf("bbc")
    cbcB = [Buf(f"cb{i}") for i in range(2)]
    sgbcB = Buf("sgbc")
    rowsB = Buf("rows")
    smxB = [Buf(f"sx{i}") for i in range(2)]
    mscB = [Buf(f"ms{i}") for i in range(2)]
    redB = [Buf(f"rd{i}") for i in range(2)]
    cmbB = [Buf(f"cm{i}") for i in range(2)]
    rlogB = [Buf(f"rl{i}") for i in range(4)]
    crstB = Buf("crst")
    constB = Buf("consts")
    stgB = [Buf(f"stg{i}") for i in range(4)]
    wrB = Buf("wrb")
    carryB = [Buf(f"cy{c}") for c in range(NCH)]
    psmB = [Buf(f"pm{i}") for i in range(4)]
    psaB = [Buf(f"pa{i}") for i in range(2)]
    pstB = Buf("pst")
    cacheB = {r["key"]: Buf(str(r["key"])) for r in regs}
    lnrowB = Buf("lnrow")
    crowB = Buf("crow")
    outB = Buf("out_t")

    psm_rot = [0]

    def next_psm():
        i = psm_rot[0] % 4
        psm_rot[0] += 1
        return i

    def acc_full(c):
        return acc[:, c * LSH + 1:c * LSH + 515]

    def acc_own(c):
        return acc[:, c * LSH + 2:c * LSH + 2 + LS]

    def acc_chunk(c):
        return acc[:, c * LSH:(c + 1) * LSH]

    def ln_full(c):
        return lnb[:, c * LSH + 1:c * LSH + 515]

    def ln_chunk(c):
        return lnb[:, c * LSH:(c + 1) * LSH]

    def ln_own(c):
        return lnb[:, c * LSH + 2:c * LSH + 2 + LS]

    def xb_ch(c):
        return xbuf[:, c * LS:(c + 1) * LS]

    def run_mms(e, mms):
        ins = None
        for (o, lt, rh, st, sp) in mms:
            ins = e.matmul(o, lt, rh, start=st, stop=sp)
        ins.then_inc(SEM["PE"], 1)

    def peg(psb, mms, reads):
        b.op("PE", (lambda mms=tuple(mms): (lambda e: run_mms(e, mms)))(),
             reads=list(reads), writes=[psb])

    def dve(emit1, reads, writes):
        def emit(e, f=emit1):
            f(e).then_inc(SEM["DVE"], 1)
        return b.op("DVE", emit, reads=list(reads), writes=list(writes))

    def act(emit1, reads, writes):
        def emit(e, f=emit1):
            f(e).then_inc(SEM["ACT"], 1)
        return b.op("ACT", emit, reads=list(reads), writes=list(writes))

    qrot = {q: 0 for q in QSLOTS}
    quseB = {q: [Buf(f"{q}use{i}") for i in range(n)] for q, n in QSLOTS.items()}

    def dma(q, pairs, reads, writes, **kw):
        k = qrot[q] % QSLOTS[q]
        qrot[q] += 1
        dom = f"{q}{k}"

        def emit(e, pairs=tuple(pairs), kw=kw, dom=dom):
            for (dst, src) in pairs:
                e.dma_start(dst, src, **kw).then_inc(SEM[dom], 16)
        return b.op(dom, emit, reads=list(reads),
                    writes=list(writes) + [quseB[q][k]],
                    inc=16 * len(pairs))

    cast_rot = [0]

    def cast_piece(ceng, dst_ap, src_ap, reads, writes):
        if ceng == "DVE":
            dve(lambda e: e.tensor_copy(dst_ap, src_ap), reads, writes)
        elif ceng == "ACT":
            act(lambda e: e.copy(dst_ap, src_ap), reads, writes)
        else:
            def emit(e):
                e.tensor_copy(dst_ap, src_ap).then_inc(SEM["PL"], 1)
            b.op("PL", emit, reads=list(reads), writes=list(writes))

    wbrot = [0]

    def wqueue(key):
        return "SP"

    def load_region(seg, key, slot_i, dslot=False):
        r = rmap[key]
        w = r["kc"] * r["cols"]
        off = r["off"]
        slot = (Dsl if dslot else Wsl)[slot_i]
        dstap = slot[:, 0:w]
        slotB = (DB if dslot else WB)[slot_i]
        cB = cacheB[key]
        q = wqueue(key)
        if seg == 0:
            ceng = ("DVE", "ACT", "PL")[cast_rot[0] % 3]
            cast_rot[0] += 1
            poff = 0
            pi = 0
            while poff < w:
                pw = min(512, w - poff)
                pq = "SP" if (pi % 2 == 0) else "ACTQ"
                pp = (0 if pq == "SP" else 2) + (pi // 2) % 2
                dma(pq, [(stg[pp][:, 0:pw],
                          wflat[:, off + poff:off + poff + pw])],
                    reads=[], writes=[stgB[pp]])
                cast_piece(ceng, slot[:, poff:poff + pw], stg[pp][:, 0:pw],
                           [stgB[pp]], [slotB])
                poff += pw
                pi += 1
            wbq = "ACTQ" if (wbrot[0] % 2) else "SP"
            wbrot[0] += 1
            dma(wbq, [(cflat[:, off:off + w], dstap)],
                reads=[slotB], writes=[cB])
        else:
            dma(q, [(dstap, cflat[:, off:off + w])], reads=[cB], writes=[slotB])
        return r

    def lhsT(slot_i, r, k, j, dslot=False):
        slot = (Dsl if dslot else Wsl)[slot_i]
        base = k * r["cols"] + j * 128
        return slot[:, base:base + 128]

    def emit_pro(e):
        e.memset(carry[:, :], 0.0)
        e.memset(epsb[:, :], EPS)
        e.memset(onesf[:, :], 1.0)
        e.memset(ones[:, :], 1.0).then_inc(SEM["DVE"], 4)
    tok = b.op("DVE", emit_pro, reads=[], writes=[constB], inc=4)
    for cc in carryB:
        cc.w = tok
    dma("ACTQ", [(bub[:, :], bu16), (bvb[:, :], bv16), (bg1b[:, :], bg1d),
                 (bg2b[:, :], bg2d), (cbb[:, :], cbd), (cwb[:, :], cwd)],
        reads=[], writes=[constB])
    dma("SP", [(stg[0][:, 0:144], wr9)], reads=[], writes=[stgB[0]])
    dve(lambda e: e.tensor_copy(wrb[:, :], stg[0][:, 0:144]),
        reads=[stgB[0]], writes=[wrB])
    dma("SP", [(stg[1][:, 0:128], identd)], reads=[], writes=[stgB[1]])
    dve(lambda e: e.tensor_copy(identW[:, :], stg[1][:, 0:128]),
        reads=[stgB[1]], writes=[wrB])

    def peg_stats(mms, reads):
        b.op("PE", (lambda mms=tuple(mms): (lambda e: run_mms(e, mms)))(),
             reads=list(reads), writes=[pstB])

    def ln_cast(which, c):
        src = acc_full if which == 1 else acc_own
        dstf = ln_full if which == 1 else ln_own
        dve(lambda e, c=c: e.tensor_copy(dstf(c), src(c)),
            reads=[accB[c]], writes=[lnB[c]])

    def layernorm(which, skip_casts=False):
        n = 514 if which == 1 else LS
        src = acc_full if which == 1 else acc_own
        dstf = ln_full if which == 1 else ln_own
        if not skip_casts:
            for c in range(NCH):
                ln_cast(which, c)
        for c in range(NCH):
            mms = [(pst[0:1, 0:512], ones[:, 0:1], dstf(c)[:, 0:512],
                    c == 0, c == NCH - 1)]
            if which == 1:
                mms.append((pst[0:1, 512:514], ones[:, 0:1],
                            dstf(c)[:, 512:514], c == 0, c == NCH - 1))
            peg_stats(mms, [lnB[c], constB])
        act(lambda e: e.copy(rows[:, 0:n], pst[0:1, 0:n]),
            reads=[pstB], writes=[rowsB])
        for c in range(NCH):
            p = c % 2
            act(lambda e, c=c, p=p: e.activation(
                xsqb[p][:, 0:n], src(c), AF.Square, bias=0.0, scale=1.0),
                reads=[accB[c]], writes=[xsqB[p]])
            mms = [(pst[0:1, 0:512], ones[:, 0:1], xsqb[p][:, 0:512],
                    c == 0, c == NCH - 1)]
            if which == 1:
                mms.append((pst[0:1, 512:514], ones[:, 0:1],
                            xsqb[p][:, 512:514], c == 0, c == NCH - 1))
            peg_stats(mms, [xsqB[p], constB])
        act(lambda e: e.copy(rows[:, LSH:LSH + n], pst[0:1, 0:n]),
            reads=[pstB], writes=[rowsB])
        S = rows[:, 0:n]
        Q = rows[:, LSH:LSH + n]
        A_ = rows[:, 2 * LSH:2 * LSH + n]
        B_ = rows[:, 3 * LSH:3 * LSH + n]
        dve(lambda e: e.tensor_mul(A_, S, S), reads=[rowsB], writes=[rowsB])
        dve(lambda e: e.scalar_tensor_tensor(Q, A_, -1.0 / D, Q, AL.mult, AL.add),
            reads=[rowsB], writes=[rowsB])
        act(lambda e: e.activation(A_, Q, AF.Ln, bias=epsb[:, 0:1], scale=1.0 / D),
            reads=[rowsB, constB], writes=[rowsB])
        act(lambda e: e.activation(Q, A_, AF.Exp, bias=0.0, scale=-0.5),
            reads=[rowsB], writes=[rowsB])
        dve(lambda e: e.scalar_tensor_tensor(B_, S, -1.0 / D, Q, AL.mult, AL.mult),
            reads=[rowsB], writes=[rowsB])
        for row_ap, dstt, dstB in ((Q, abc, abcB), (B_, bbc, bbcB)):
            mm1 = [(psa[0][:, 0:512], onesf[0:1, :], row_ap[0:1, 0:512],
                    True, True)]
            b.op("PE", (lambda mms=tuple(mm1): (lambda e: run_mms(e, mms)))(),
                 reads=[rowsB, constB], writes=[psaB[0]])
            act(lambda e, dstt=dstt: e.copy(dstt[:, 0:512], psa[0][:, 0:512]),
                reads=[psaB[0]], writes=[dstB])
            if n > 512:
                mm2 = [(psa[1][:, 0:n - 512], onesf[0:1, :],
                        row_ap[0:1, 512:n], True, True)]
                b.op("PE", (lambda mms=tuple(mm2): (lambda e: run_mms(e, mms)))(),
                     reads=[rowsB, constB], writes=[psaB[1]])
                act(lambda e, dstt=dstt, n=n: e.copy(dstt[:, 512:n],
                                                     psa[1][:, 0:n - 512]),
                    reads=[psaB[1]], writes=[dstB])
        for c in range(NCH):
            p = c % 2
            dve(lambda e, c=c, p=p: e.tensor_mul(
                tmpb[p][:, 0:n], src(c), abc[:, 0:n]),
                reads=[accB[c], abcB], writes=[tmpB[p]])
            dve(lambda e, c=c, p=p: e.tensor_add(
                dstf(c), tmpb[p][:, 0:n], bbc[:, 0:n]),
                reads=[tmpB[p], bbcB], writes=[lnB[c]])

    def dn_load(seg, g, si):
        return load_region(seg, ("dn", g, si), si % 2, dslot=True)

    def emit_down(seg, g, smap):
        gpar = g % 2
        for dc in range(NCH):
            si = dc // 2
            dsl_i, rdn = smap[("dn", si)]
            pa_i = dc % 2
            mms = [(psa[pa_i][:, :], lhsT(dsl_i, rdn, k, dc % 2, dslot=True),
                    Abuf[gpar][:, k * LS:(k + 1) * LS],
                    k == 0, k == IGC - 1) for k in range(IGC)]
            b.op("PE", (lambda mms=tuple(mms): (lambda e: run_mms(e, mms)))(),
                 reads=[DB[dsl_i], AB[gpar]], writes=[psaB[pa_i]])
            dve(lambda e, dc=dc, pa_i=pa_i: e.tensor_add(
                acc_own(dc), acc_own(dc), psa[pa_i][:, :]),
                reads=[psaB[pa_i]], writes=[accB[dc]])
            if dc % 2 == 1 and si + 2 <= 7:
                smap[("dn", si + 2)] = ((si + 2) % 2, dn_load(seg, g, si + 2))


    def emit_softmax(p, t, nm, sm, rs, m1, m2, cpairs):
        rl = rlog[t]
        dve(lambda e: e.reduce_max(nm, rl[:, 0:8], axis=AX, negate=True),
            reads=[rlogB[t]], writes=[redB[p]])
        act(lambda e: e.activation(smx[p][:, :], rl[:, 0:8], AF.Exp, bias=nm,
                                   scale=1.0, accum_out=sm),
            reads=[rlogB[t], redB[p]], writes=[smxB[p], redB[p]])
        act(lambda e: e.activation(cmb[p][:, 8:9], rl[:, 8:9], AF.Sigmoid,
                                   bias=0.0, scale=1.0),
            reads=[rlogB[t]], writes=[cmbB[p]])
        dve(lambda e: e.reciprocal(rs, sm), reads=[redB[p]], writes=[redB[p]])
        dve(lambda e: e.tensor_scalar(smx[p][:, :], smx[p][:, :], rs, None,
                                      AL.mult),
            reads=[smxB[p], redB[p]], writes=[smxB[p]])
        dve(lambda e: e.reduce_max(m1, smx[p][:, :], axis=AX),
            reads=[smxB[p]], writes=[redB[p]])
        dve(lambda e: e.tensor_scalar(msc[p][:, :], smx[p][:, :], m1, None,
                                      AL.is_lt),
            reads=[smxB[p], redB[p]], writes=[mscB[p]])
        dve(lambda e: e.tensor_mul(msc[p][:, :], smx[p][:, :], msc[p][:, :]),
            reads=[smxB[p], mscB[p]], writes=[mscB[p]])
        dve(lambda e: e.reduce_max(m2, msc[p][:, :], axis=AX),
            reads=[mscB[p]], writes=[redB[p]])
        dve(lambda e: e.tensor_scalar(msc[p][:, :], smx[p][:, :], m2, None,
                                      AL.is_ge),
            reads=[smxB[p], redB[p]], writes=[mscB[p]])
        dve(lambda e: e.tensor_mul(cmb[p][:, 0:8], smx[p][:, :], msc[p][:, :]),
            reads=[smxB[p], mscB[p]], writes=[cmbB[p]])
        ti = next_psm()
        tview = psm[ti][:, :].bitcast(BF16)
        def emit_tr(e, p=p, ti=ti, tview=tview):
            e.transpose(tview[0:9, 0:128], cmb[p][:, 0:9],
                        identW[:, :]).then_inc(SEM["PE"], 1)
        b.op("PE", emit_tr, reads=[cmbB[p], constB], writes=[psmB[ti]])
        t4 = cpairs
        tt = t4[0]
        act(lambda e, ti=ti, tview=tview, tt=tt: e.copy(
            crst[:, tt * 128:(tt + 1) * 128], tview[0:9, 0:128]),
            reads=[psmB[ti]], writes=[crstB])

    for seg in range(NSEG):
        l0 = seg * LS
        pairs = []
        for c in range(NCH):
            sa = max(l0 - 1, 0)
            sbnd = min(l0 + LS + 1, L)
            doff = 2 - (l0 - sa)
            pairs.append((acc[:, c * LSH + doff:c * LSH + doff + (sbnd - sa)],
                          xt[c * 128:(c + 1) * 128, sa:sbnd]))
        for gq in range(4):
            dma("ACTQ", pairs[gq * 4:(gq + 1) * 4], reads=[],
                writes=accB[gq * 4:(gq + 1) * 4])
        if seg == 0:
            dve(lambda e: e.memset(acc[:, 1:NCH * LSH:LSH], 0.0),
                reads=[], writes=accB)
        if seg == NSEG - 1:
            dve(lambda e: e.memset(acc[:, 514:NCH * LSH:LSH], 0.0),
                reads=[], writes=accB)

        layernorm(1)

        acyc = [0]

        def aslot():
            i = acyc[0] % 3
            acyc[0] += 1
            return i
        rg2 = load_region(seg, ("wg2",), 3)
        rg1 = load_region(seg, ("wg1",), aslot())

        for c in range(NCH):
            p = c % 2
            w0 = cwb[:, 3 * c:3 * c + 1]
            w1 = cwb[:, 3 * c + 1:3 * c + 2]
            w2 = cwb[:, 3 * c + 2:3 * c + 3]
            cb_ = cbb[:, c:c + 1]
            lf = ln_chunk(c)
            dve(lambda e, p=p, w0=w0, cb_=cb_, lf=lf: e.tensor_scalar(
                tmpb[p][:, 0:LS], lf[:, 1:513], w0, cb_, AL.mult, AL.add),
                reads=[lnB[c], constB], writes=[tmpB[p]])
            dve(lambda e, p=p, w1=w1, lf=lf: e.scalar_tensor_tensor(
                tmpb[p][:, 0:LS], lf[:, 2:514], w1, tmpb[p][:, 0:LS],
                AL.mult, AL.add),
                reads=[lnB[c], constB, tmpB[p]], writes=[tmpB[p]])
            dve(lambda e, p=p, w2=w2, lf=lf: e.scalar_tensor_tensor(
                tmpb[p][:, 0:LS], lf[:, 3:515], w2, tmpb[p][:, 0:LS],
                AL.mult, AL.add),
                reads=[lnB[c], constB, tmpB[p]], writes=[tmpB[p]])
            dve(lambda e, c=c, p=p: e.tensor_add(
                acc_own(c), acc_own(c), tmpb[p][:, 0:LS]),
                reads=[tmpB[p]], writes=[accB[c]])

        for hc in range(4):
            pi = next_psm()
            mms = [(psm[pi][:, :], lhsT(0, rg1, k, hc), ln_own(k),
                    k == 0, k == NCH - 1) for k in range(NCH)]
            peg(psmB[pi], mms, [WB[0]] + lnB)
            act(lambda e, hc=hc, pi=pi: e.activation(
                hbuf[:, hc * LS:(hc + 1) * LS], psm[pi][:, :], AF.Relu,
                bias=bg1b[:, hc:hc + 1], scale=1.0),
                reads=[psmB[pi], constB], writes=[hB])

        cur = {}
        for rc in range(NCH):
            blk = rc // 4
            if rc % 4 == 0:
                su = aslot()
                ru = load_region(seg, ("wu", blk), su)
                sv = aslot()
                rv = load_region(seg, ("wv", blk), sv)
                cur[blk] = (su, ru, sv, rv)
            su, ru, sv, rv = cur[blk]
            p = rc % 2
            pg = next_psm()
            mms = [(psm[pg][:, :], lhsT(3, rg2, hk, rc),
                    hbuf[:, hk * LS:(hk + 1) * LS], hk == 0, hk == 3)
                   for hk in range(4)]
            peg(psmB[pg], mms, [WB[3], hB])
            act(lambda e, rc=rc, pg=pg, p=p: e.activation(
                gtb[p][:, :], psm[pg][:, :], AF.Sigmoid,
                bias=bg2b[:, rc:rc + 1], scale=1.0),
                reads=[psmB[pg], constB], writes=[gtB[p]])
            pu = next_psm()
            mms = [(psm[pu][:, :], lhsT(su, ru, k, rc % 4), ln_own(k),
                    k == 0, k == NCH - 1) for k in range(NCH)]
            peg(psmB[pu], mms, [WB[su]] + lnB)
            dve(lambda e, rc=rc, pu=pu, p=p: e.scalar_tensor_tensor(
                ugb[p][:, :], psm[pu][:, :], bub[:, rc:rc + 1], gtb[p][:, :],
                AL.add, AL.mult),
                reads=[psmB[pu], constB, gtB[p]], writes=[ugB[p]])
            pv = next_psm()
            mms = [(psm[pv][:, :], lhsT(sv, rv, k, rc % 4), ln_own(k),
                    k == 0, k == NCH - 1) for k in range(NCH)]
            peg(psmB[pv], mms, [WB[sv]] + lnB)
            dve(lambda e, rc=rc, pv=pv, p=p: e.scalar_tensor_tensor(
                vgb[p][:, :], psm[pv][:, :], bvb[:, rc:rc + 1], gtb[p][:, :],
                AL.add, AL.mult),
                reads=[psmB[pv], constB, gtB[p]], writes=[vgB[p]])
            dve(lambda e, rc=rc, p=p: e.tensor_tensor_scan(
                kcb[p][:, :], ugb[p][:, :], ugb[p][:, :],
                carry[:, rc:rc + 1], AL.add, AL.bypass),
                reads=[ugB[p], carryB[rc]], writes=[kcB[p]])
            dve(lambda e, rc=rc, p=p: e.tensor_copy(
                carry[:, rc:rc + 1], kcb[p][:, LS - 1:LS]),
                reads=[kcB[p]], writes=[carryB[rc]])
            dve(lambda e, rc=rc, p=p: e.tensor_mul(
                xb_ch(rc), kcb[p][:, :], vgb[p][:, :]),
                reads=[kcB[p], vgB[p]], writes=[xbB[rc]])

        for dc in range(NCH):
            blk = dc // 4
            if dc % 4 == 0:
                si_uv = aslot()
                ruv = load_region(seg, ("wu", blk), si_uv)
                cur[("uv", blk)] = (si_uv, ruv)
            si_uv, ruv = cur[("uv", blk)]
            pa_i = dc % 2
            mms = [(psa[pa_i][:, :], lhsT(si_uv, ruv, k, dc % 4), xb_ch(k),
                    k == 0, k == NCH - 1) for k in range(NCH)]
            b.op("PE", (lambda mms=tuple(mms): (lambda e: run_mms(e, mms)))(),
                 reads=[WB[si_uv]] + xbB, writes=[psaB[pa_i]])
            dve(lambda e, dc=dc, pa_i=pa_i: e.tensor_add(
                acc_own(dc), acc_own(dc), psa[pa_i][:, :]),
                reads=[psaB[pa_i]], writes=[accB[dc]])
            ln_cast(2, dc)

        layernorm(2, skip_casts=True)

        router_tail = []
        for t in range(4):
            p = t % 2
            pr_i = next_psm()
            mms = [(psm[pr_i][:, 0:9],
                    lnb[:, k * LSH + 2 + t * 128:k * LSH + 2 + (t + 1) * 128],
                    wrb[:, k * 9:(k + 1) * 9], k == 0, k == NCH - 1)
                   for k in range(NCH)]
            peg(psmB[pr_i], mms, lnB + [wrB])
            act(lambda e, t=t, pr_i=pr_i: e.copy(rlog[t][:, 0:9],
                                                 psm[pr_i][:, 0:9]),
                reads=[psmB[pr_i]], writes=[rlogB[t]])
            nm = red[p][:, 0:1]
            sm = red[p][:, 1:2]
            rs = red[p][:, 2:3]
            m1 = red[p][:, 3:4]
            m2 = red[p][:, 4:5]
            router_tail.append(
                (lambda p=p, t=t, nm=nm, sm=sm, rs=rs, m1=m1, m2=m2:
                 emit_softmax(p, t, nm, sm, rs, m1, m2, (t,))))

        def emit_x2s():
            dma("ROWQ", [(crow[0:9, l0:l0 + LS], crst[:, :])],
                reads=[crstB], writes=[crowB])
            dma("ROWQ", [(sgbc[:, :],
                          crow[8:9, l0:l0 + LS].broadcast_to([128, LS]))],
                reads=[crowB], writes=[sgbcB])
            for c in range(NCH):
                def emit_xs(e, c=c):
                    e.tensor_mul(xb_ch(c), ln_own(c),
                                 sgbc[:, :]).then_inc(SEM["PL"], 1)
                b.op("PL", emit_xs, reads=[lnB[c], sgbcB], writes=[xbB[c]])

        fcyc = [0]

        def fslot():
            i = [3, 0, 1, 2][fcyc[0] % 4]
            fcyc[0] += 1
            return i
        gsmaps = {}
        for g in range(NGRP):
            smap = {}
            gsmaps[g] = smap
            if g >= 1:
                pm1 = gsmaps[g - 1]
                pm1[("dn", 0)] = (0, dn_load(seg, g - 1, 0))
                pm1[("dn", 1)] = (1, dn_load(seg, g - 1, 1))
            gpar = g % 2
            for ii in range(IGC):
                si = ii // 4
                if ii % 4 == 0:
                    sl = fslot()
                    smap[("gate", si)] = (sl, load_region(seg, ("gate", g, si), sl))
                sl, r_ = smap[("gate", si)]
                pg = next_psm()
                mms = [(psm[pg][:, :], lhsT(sl, r_, k, ii - 4 * si), ln_own(k),
                        k == 0, k == NCH - 1) for k in range(NCH)]
                peg(psmB[pg], mms, [WB[sl]] + lnB)
                act(lambda e, ii=ii, pg=pg, gpar=gpar: e.activation(
                    Abuf[gpar][:, ii * LS:(ii + 1) * LS], psm[pg][:, :],
                    AF.Silu, bias=0.0, scale=1.0),
                    reads=[psmB[pg]], writes=[AB[gpar]])
            if g == 0:
                for f in router_tail:
                    f()
                emit_x2s()
            for ii in range(IGC):
                si = ii // 4
                if ii % 4 == 0:
                    sl = fslot()
                    smap[("up", si)] = (sl, load_region(seg, ("up", g, si), sl))
                sl, r_ = smap[("up", si)]
                pc = next_psm()
                mms = [(psm[pc][:, :], lhsT(sl, r_, k, ii - 4 * si), xb_ch(k),
                        k == 0, k == NCH - 1) for k in range(NCH)]
                b.op("PE", (lambda mms=tuple(mms): (lambda e: run_mms(e, mms)))(),
                     reads=[WB[sl]] + xbB, writes=[psmB[pc]])
                dve(lambda e, ii=ii, pc=pc, gpar=gpar: e.tensor_mul(
                    Abuf[gpar][:, ii * LS:(ii + 1) * LS],
                    psm[pc][:, :], Abuf[gpar][:, ii * LS:(ii + 1) * LS]),
                    reads=[psmB[pc], AB[gpar]], writes=[AB[gpar]])
            if 4 <= g + 1 < NGRP:
                e_next = g + 1 - 4
                cp = e_next % 2
                dma("ROWQ", [(cbc[cp][:, :],
                              crow[e_next:e_next + 1, l0:l0 + LS]
                              .broadcast_to([128, LS]))],
                    reads=[crowB], writes=[cbcB[cp]])
                for c in range(NCH):
                    def emit_xc(e, c=c, cp=cp):
                        e.tensor_mul(xb_ch(c), ln_own(c), cbc[cp][:, :]).then_inc(SEM["PL"], 1)
                    b.op("PL", emit_xc, reads=[lnB[c], cbcB[cp]], writes=[xbB[c]])
            if g >= 1:
                emit_down(seg, g - 1, gsmaps[g - 1])
        gsmaps[NGRP - 1][("dn", 0)] = (0, dn_load(seg, NGRP - 1, 0))
        gsmaps[NGRP - 1][("dn", 1)] = (1, dn_load(seg, NGRP - 1, 1))
        emit_down(seg, NGRP - 1, gsmaps[NGRP - 1])

        spairs = [(out_t[c * 128:(c + 1) * 128, l0:l0 + LS], acc_own(c))
                  for c in range(NCH)]
        for gq in range(4):
            dma("ACTQ", spairs[gq * 4:(gq + 1) * 4],
                reads=accB[gq * 4:(gq + 1) * 4], writes=[outB])

    b.finish()
    with nc.Block() as block:
        b.realize(block)
    return nc


def kernel(**inputs):
    regs, TOTW = make_regions()
    wflat = pack_host(inputs, regs, TOTW)
    bu16 = np.ascontiguousarray(inputs["L0_attn_bu"].reshape(16, 128).T)
    bv16 = np.ascontiguousarray(inputs["L0_attn_bv"].reshape(16, 128).T)
    bg1d = np.ascontiguousarray(inputs["L0_attn_bg1"].reshape(4, 128).T)
    bg2d = np.ascontiguousarray(inputs["L0_attn_bg2"].reshape(16, 128).T)
    cbd = np.ascontiguousarray(inputs["L0_attn_conv_b"].reshape(16, 128).T)
    cwd = np.ascontiguousarray(
        inputs["L0_attn_conv_w"].reshape(16, 128, 3)
        .transpose(1, 0, 2).reshape(128, 48))
    wr9 = np.ascontiguousarray(
        np.concatenate([inputs["L0_ffn_W_router"], inputs["L0_ffn_W_sgate"]], 0)
        .T.reshape(16, 128, 9).transpose(1, 0, 2).reshape(128, 144))
    ident = np.eye(128, dtype=np.float32)
    x = inputs["L0_attn_x"]

    nc = build_program()
    in_maps = []
    for c in range(8):
        in_maps.append(dict(
            xt=np.ascontiguousarray(x[c].T), wflat=wflat, bu16=bu16,
            bv16=bv16, bg1d=bg1d, bg2d=bg2d, cbd=cbd, cwd=cwd, wr9=wr9,
            identd=ident))
    res = run_bass_kernel_spmd(nc, in_maps, core_ids=list(range(8)))
    out = np.stack([np.ascontiguousarray(r["out_t"].T) for r in res.results])
    return out.astype(np.float32)
